# version 01 (final)
# baseline (speedup 1.0000x reference)
import math
import numpy as np
import concourse.bass as bass
import concourse.mybir as mybir
from concourse.bass_utils import run_bass_kernel_spmd

F32 = mybir.dt.float32
BF16 = mybir.dt.bfloat16
I32 = mybir.dt.int32
AF = mybir.ActivationFunctionType
OP = mybir.AluOpType
AX = mybir.AxisListType.X

B, T, D = 2, 8192, 1024
NCORE = 8
TC = 2048
HALO = 512

ENGINES = ("sync", "vector", "scalar", "tensor", "gpsimd")
SEM_NAMES = ["d_sync", "d_gpsimd", "d_vector", "d_scalar", "pv", "ps", "pt",
             "pp", "bar", "cc"]


class Ctx:
    def __init__(self, nc):
        self.nc = nc
        self.q = {e: [] for e in ENGINES}
        self.sems = {}
        self.cnt = {}

    def add_sem(self, name, handle):
        self.sems[name] = handle
        self.cnt[name] = 0

    def do(self, eng, fn):
        self.q[eng].append(lambda e, fn=fn: fn(e))

    def sig(self, eng, fn, sem=None, amt=1):
        sem = sem or {"vector": "pv", "scalar": "ps", "tensor": "pt",
                      "gpsimd": "pp"}[eng]
        s = self.sems[sem]
        self.cnt[sem] += amt

        def wrapped(e, fn=fn, s=s, amt=amt):
            fn(e).then_inc(s, amt)
        self.q[eng].append(wrapped)
        return self.cnt[sem]

    def dma(self, eng, out, in_, **kw):
        sem = "d_" + eng
        return self.sig(eng, lambda e: e.dma_start(out=out, in_=in_, **kw),
                        sem, 16)

    def wait(self, eng, sem, val):
        s = self.sems[sem]
        self.q[eng].append(lambda e, s=s, val=val: e.wait_ge(s, val))

    def tok(self, sem):
        return self.cnt[sem]

    def barrier(self, engines=ENGINES):
        s = self.sems["bar"]
        self.cnt["bar"] += len(engines)
        tgt = self.cnt["bar"]
        for eng in engines:
            self.q[eng].append(lambda e, s=s: e.sem_inc(s, 1))
            self.q[eng].append(lambda e, s=s, tgt=tgt: e.wait_ge(s, tgt))

    def replay(self, block):
        for name in ENGINES:
            getattr(block, name)(self._runner(name))

    def _runner(self, name):
        queue = self.q[name]

        def run(engobj):
            for fn in queue:
                fn(engobj)
        return run


class Arena:
    def __init__(self, ap, nbytes):
        self.ap = ap
        self.nbytes = nbytes
        self.off = 0
        self.marks = []
        self.hi = 0

    def alloc(self, shape, dtype):
        assert shape[0] == 128
        free = int(np.prod(shape[1:]))
        esz = mybir.dt.size(dtype)
        self.off = (self.off + 3) // 4 * 4
        start = self.off
        self.off += free * esz
        self.hi = max(self.hi, self.off)
        assert self.off <= self.nbytes, f"arena overflow {self.off}>{self.nbytes}"
        ap = self.ap[:, start:start + free * esz].bitcast(dtype)
        if len(shape) > 2:
            dims = " ".join(f"d{i}" for i in range(len(shape) - 1))
            ap = ap.rearrange(f"p ({dims}) -> p {dims}",
                              **{f"d{i}": int(shape[i + 1])
                                 for i in range(len(shape) - 1)})
        return ap

    def mark(self):
        self.marks.append(self.off)

    def release(self):
        self.off = self.marks.pop()


def _prog(arena_bytes, psums):
    import contextlib
    nc = bass.Bass("TRN2", target_bir_lowering=False, debug=False,
                   detect_race_conditions=False)
    est = contextlib.ExitStack()
    arena_t = est.enter_context(
        nc.sbuf_tensor("arena", [128, arena_bytes], mybir.dt.uint8))
    ctx = Ctx(nc)
    for s in SEM_NAMES:
        ctx.add_sem(s, est.enter_context(nc.semaphore(s)))
    ps = {}
    for pname, pshape, pdt in psums:
        ps[pname] = est.enter_context(nc.psum_tensor(pname, list(pshape), pdt))
    return nc, ctx, Arena(arena_t[:, :], arena_bytes), ps, est



def _alibi_slopes(h):
    def pow2(n):
        start = 2 ** (-2 ** (-(math.log2(n) - 3)))
        return [start * start ** i for i in range(n)]
    if math.log2(h).is_integer():
        return np.array(pow2(h), np.float32)
    closest = 2 ** math.floor(math.log2(h))
    sl = pow2(closest)
    extra = pow2(2 * closest)
    sl += extra[0::2][:h - closest]
    return np.array(sl, np.float32)


def l0_wmask():
    sl = _alibi_slopes(16).astype(np.float64)
    i = np.arange(128)[:, None].astype(np.float64)
    j = np.arange(640)[None, :].astype(np.float64)
    d = i - j
    band = (d <= 0) & (d >= -512)
    W = np.where(band[None], np.exp(sl[:, None, None] * d[None]), 0.0)
    return np.ascontiguousarray(W.astype(np.float32))



def build_W0():
    psums = ()
    nc, ctx, ar, ps, est = _prog(150000, psums)
    keys = nc.dram_tensor("keys", [4, 1024, 512], F32, kind="ExternalInput").ap()
    values = nc.dram_tensor("values", [4, 512, 1024], F32, kind="ExternalInput").ap()
    wg0 = nc.dram_tensor("wg0", [384, 1024], I32, kind="ExternalInput").ap()
    wu0 = nc.dram_tensor("wu0", [384, 1024], I32, kind="ExternalInput").ap()
    wd0 = nc.dram_tensor("wd0", [128, 3072], I32, kind="ExternalInput").ap()
    wg2 = nc.dram_tensor("wg2", [352, 1024], F32, kind="ExternalInput").ap()
    wu2 = nc.dram_tensor("wu2", [352, 1024], F32, kind="ExternalInput").ap()
    wd2 = nc.dram_tensor("wd2", [128, 2816], F32, kind="ExternalInput").ap()
    keys_o = nc.dram_tensor("keys_o", [4, 1024, 512], BF16, kind="ExternalOutput").ap()
    values_o = nc.dram_tensor("values_o", [4, 512, 1024], BF16, kind="ExternalOutput").ap()
    wg0_o = nc.dram_tensor("wg0_o", [384, 1024], BF16, kind="ExternalOutput").ap()
    wu0_o = nc.dram_tensor("wu0_o", [384, 1024], BF16, kind="ExternalOutput").ap()
    wd0_o = nc.dram_tensor("wd0_o", [128, 3072], BF16, kind="ExternalOutput").ap()
    wg2_o = nc.dram_tensor("wg2_o", [352, 1024], BF16, kind="ExternalOutput").ap()
    wu2_o = nc.dram_tensor("wu2_o", [352, 1024], BF16, kind="ExternalOutput").ap()
    wd2_o = nc.dram_tensor("wd2_o", [128, 2816], BF16, kind="ExternalOutput").ap()

    with est:
        CMAX = 3072
        raw = ar.alloc([128, CMAX], F32)
        outb = ar.alloc([128, CMAX], BF16)
        absn = ar.alloc([128, CMAX], F32)
        acc = ar.alloc([128, CMAX], F32)
        stp = ar.alloc([128, CMAX], F32)
        sgn = ar.alloc([128, CMAX], F32)
        scl = ar.alloc([128, CMAX // 16], F32)
        rs = ar.alloc([128, CMAX // 16], F32)

        rawi = raw.bitcast(I32)

        def cast_copy(src, dst, rows, cols, scale=None):
            isint = src.dtype == I32
            for r0 in range(0, rows, 128):
                rr = min(128, rows - r0)
                ld = rawi if isint else raw
                t = ctx.dma("sync", ld[0:rr, 0:cols], src[r0:r0 + rr, :])
                ctx.wait("vector", "d_sync", t)
                rdsrc = rawi if isint else raw
                if scale is None:
                    tv = ctx.sig("vector", lambda v, rr=rr, rdsrc=rdsrc:
                                 v.tensor_copy(
                                     out=outb[0:rr, 0:cols], in_=rdsrc[0:rr, 0:cols]))
                else:
                    tv = ctx.sig("vector", lambda v, rr=rr, rdsrc=rdsrc:
                                 v.tensor_scalar(
                                     out=outb[0:rr, 0:cols], in0=rdsrc[0:rr, 0:cols],
                                     scalar1=scale, scalar2=None, op0=OP.mult))
                ctx.wait("gpsimd", "pv", tv)
                tg = ctx.dma("gpsimd", dst[r0:r0 + rr, :], outb[0:rr, 0:cols])
                ctx.wait("vector", "d_gpsimd", tg)
                ctx.wait("sync", "d_gpsimd", tg)

        MIDS = [0.25, 0.75, 1.25, 1.75, 2.5, 3.5, 5.0, 5.0]
        DGS = [0.5, 0.5, 0.5, 0.5, 1.0, 1.0, 1.0, 1.0]

        def qdq(src, dst, rows, cols):
            nb = cols // 16
            for r0 in range(0, rows, 128):
                rr = min(128, rows - r0)
                t = ctx.dma("sync", raw[0:rr, 0:cols], src[r0:r0 + rr, :])
                ctx.wait("vector", "d_sync", t)
                ctx.wait("scalar", "d_sync", t)
                xb = raw[0:rr, 0:cols].rearrange("p (nb k) -> p nb k", k=16)
                ctx.sig("vector", lambda v, rr=rr, nb=nb, xb=xb: v.tensor_reduce(
                    out=scl[0:rr, 0:nb], in_=xb, axis=AX, op=OP.max,
                    apply_absolute_value=True))
                ctx.sig("vector", lambda v, rr=rr, nb=nb: v.tensor_scalar(
                    out=scl[0:rr, 0:nb], in0=scl[0:rr, 0:nb],
                    scalar1=1.0 / 6.0, scalar2=1e-12, op0=OP.mult, op1=OP.max))
                ctx.sig("vector", lambda v, rr=rr, nb=nb: v.reciprocal(
                    out=rs[0:rr, 0:nb], in_=scl[0:rr, 0:nb]))
                ta_ = ctx.sig("scalar", lambda s, rr=rr: s.activation(
                    out=absn[0:rr, 0:cols], in_=raw[0:rr, 0:cols], func=AF.Abs))
                ctx.wait("vector", "ps", ta_)
                tv = ctx.sig("vector", lambda v, rr=rr, nb=nb: v.tensor_tensor(
                    out=absn[0:rr, 0:cols].rearrange("p (nb k) -> p nb k", k=16),
                    in0=absn[0:rr, 0:cols].rearrange("p (nb k) -> p nb k", k=16),
                    in1=rs[0:rr, 0:nb, None].broadcast_to([rr, nb, 16]),
                    op=OP.mult))
                ts_ = ctx.sig("scalar", lambda s, rr=rr: s.activation(
                    out=sgn[0:rr, 0:cols], in_=raw[0:rr, 0:cols], func=AF.Sign))
                first = True
                for mid, dg in zip(MIDS, DGS):
                    if first:
                        ctx.sig("vector", lambda v, rr=rr, mid=mid, dg=dg:
                                v.tensor_scalar(
                                    out=acc[0:rr, 0:cols], in0=absn[0:rr, 0:cols],
                                    scalar1=mid, scalar2=dg,
                                    op0=OP.is_gt, op1=OP.mult))
                        first = False
                    else:
                        ctx.sig("vector", lambda v, rr=rr, mid=mid, dg=dg:
                                v.tensor_scalar(
                                    out=stp[0:rr, 0:cols], in0=absn[0:rr, 0:cols],
                                    scalar1=mid, scalar2=dg,
                                    op0=OP.is_gt, op1=OP.mult))
                        ctx.sig("vector", lambda v, rr=rr: v.tensor_tensor(
                            out=acc[0:rr, 0:cols], in0=stp[0:rr, 0:cols],
                            in1=acc[0:rr, 0:cols], op=OP.add))
                ctx.wait("vector", "ps", ts_)
                ctx.sig("vector", lambda v, rr=rr: v.tensor_tensor(
                    out=acc[0:rr, 0:cols], in0=sgn[0:rr, 0:cols],
                    in1=acc[0:rr, 0:cols], op=OP.mult))
                tv = ctx.sig("vector", lambda v, rr=rr, nb=nb: v.tensor_tensor(
                    out=outb[0:rr, 0:cols].rearrange("p (nb k) -> p nb k", k=16),
                    in0=acc[0:rr, 0:cols].rearrange("p (nb k) -> p nb k", k=16),
                    in1=scl[0:rr, 0:nb, None].broadcast_to([rr, nb, 16]),
                    op=OP.mult))
                ctx.wait("gpsimd", "pv", tv)
                tg = ctx.dma("gpsimd", dst[r0:r0 + rr, :], outb[0:rr, 0:cols])
                ctx.wait("vector", "d_gpsimd", tg)
                ctx.wait("scalar", "d_gpsimd", tg)
                ctx.wait("sync", "d_gpsimd", tg)

        for e in range(4):
            cast_copy(keys[e], keys_o[e], 1024, 512)
            cast_copy(values[e], values_o[e], 512, 1024)
        cast_copy(wg0, wg0_o, 384, 1024, scale=1.0 / 127.0)
        cast_copy(wu0, wu0_o, 384, 1024, scale=1.0 / 127.0)
        cast_copy(wd0, wd0_o, 128, 3072, scale=1.0 / 127.0)
        qdq(wg2, wg2_o, 352, 1024)
        qdq(wu2, wu2_o, 352, 1024)
        qdq(wd2, wd2_o, 128, 2816)

        with nc.Block() as block:
            ctx.replay(block)
    return nc


def run_W0(inp):
    nc = build_W0()
    in_maps = []
    for c in range(NCORE):
        m = {
            "keys": np.ascontiguousarray(inp["L1_ffn_keys"][c * 4:(c + 1) * 4]),
            "values": np.ascontiguousarray(inp["L1_ffn_values"][c * 4:(c + 1) * 4]),
            "wg0": np.ascontiguousarray(inp["L0_ffn_w_gate"][c * 384:(c + 1) * 384]),
            "wu0": np.ascontiguousarray(inp["L0_ffn_w_up"][c * 384:(c + 1) * 384]),
            "wd0": np.ascontiguousarray(inp["L0_ffn_w_down"][c * 128:(c + 1) * 128]),
            "wg2": np.ascontiguousarray(inp["L2_ffn_w_gate"][c * 352:(c + 1) * 352]),
            "wu2": np.ascontiguousarray(inp["L2_ffn_w_up"][c * 352:(c + 1) * 352]),
            "wd2": np.ascontiguousarray(inp["L2_ffn_w_down"][c * 128:(c + 1) * 128]),
        }
        in_maps.append(m)
    res = run_bass_kernel_spmd(nc, in_maps, core_ids=list(range(NCORE)))
    r = res.results
    cat = lambda k: np.concatenate([r[c][k] for c in range(NCORE)], 0)
    return {
        "keys_bf": cat("keys_o"),
        "values_bf": cat("values_o"),
        "wg0T": np.ascontiguousarray(cat("wg0_o").T),
        "wu0T": np.ascontiguousarray(cat("wu0_o").T),
        "wd0T": np.ascontiguousarray(cat("wd0_o").T),
        "wg2T": np.ascontiguousarray(cat("wg2_o").T),
        "wu2T": np.ascontiguousarray(cat("wu2_o").T),
        "wd2T": np.ascontiguousarray(cat("wd2_o").T),
    }



def emit_ln_fm(ctx, PS_R, PS_BC, onesc, rowbuf, xld, x2s, qsc,
               src_dram, src_c0, xn_write, ntc, eps, tcw=512, D_=1024):
    for tcol in range(ntc):
        c0 = src_c0 + tcol * tcw
        for dt in range(8):
            t = ctx.dma("sync", xld[:, dt % 2, :],
                        src_dram[dt * 128:(dt + 1) * 128, c0:c0 + tcw])
            ctx.wait("tensor", "d_sync", t)
            ctx.wait("scalar", "d_sync", t)
            tmm = ctx.sig("tensor", lambda t_, dt=dt: t_.matmul(
                PS_R[0:1, 0:tcw], onesc[:, 0:1], xld[:, dt % 2, :],
                start=(dt == 0), stop=(dt == 7)))
            tsq = ctx.sig("scalar", lambda s, dt=dt: s.activation(
                out=x2s[:, :], in_=xld[:, dt % 2, :], func=AF.Square))
            ctx.wait("tensor", "ps", tsq)
            tm2 = ctx.sig("tensor", lambda t_, dt=dt: t_.matmul(
                PS_BC[0:1, 0:tcw], onesc[:, 0:1], x2s[:, :],
                start=(dt == 0), stop=(dt == 7)))
            ctx.wait("sync", "pt", tm2)
            ctx.wait("scalar", "pt", tm2)
        tmm_last = ctx.tok("pt")
        ctx.wait("vector", "pt", tmm_last)
        tv1 = ctx.sig("vector", lambda v: v.tensor_scalar(
            out=rowbuf[0:1, 0:tcw], in0=PS_R[0:1, 0:tcw],
            scalar1=-1.0 / D_, scalar2=None, op0=OP.mult))
        ctx.wait("scalar", "pv", tv1)
        ts1 = ctx.sig("scalar", lambda s: s.activation(
            out=rowbuf[0:1, 512:512 + tcw], in_=rowbuf[0:1, 0:tcw],
            func=AF.Square))
        ctx.wait("vector", "ps", ts1)
        ctx.sig("vector", lambda v: v.scalar_tensor_tensor(
            out=rowbuf[0:1, 1024:1024 + tcw], in0=PS_BC[0:1, 0:tcw],
            scalar=1.0 / D_, in1=rowbuf[0:1, 512:512 + tcw],
            op0=OP.mult, op1=OP.subtract))
        tv2 = ctx.sig("vector", lambda v: v.tensor_scalar(
            out=rowbuf[0:1, 1024:1024 + tcw],
            in0=rowbuf[0:1, 1024:1024 + tcw],
            scalar1=eps, scalar2=None, op0=OP.add))
        ctx.wait("scalar", "pv", tv2)
        ts2 = ctx.sig("scalar", lambda s: s.activation(
            out=rowbuf[0:1, 1536:1536 + tcw],
            in_=rowbuf[0:1, 1024:1024 + tcw], func=AF.Sqrt))
        ctx.wait("vector", "ps", ts2)
        tv3 = ctx.sig("vector", lambda v: v.reciprocal(
            out=rowbuf[0:1, 1024:1024 + tcw],
            in_=rowbuf[0:1, 1536:1536 + tcw]))
        ctx.wait("tensor", "pv", tv3)
        tb1 = ctx.sig("tensor", lambda t_: t_.matmul(
            PS_BC[:, 0:tcw], onesc[0:1, :], rowbuf[0:1, 0:tcw],
            start=True, stop=True))
        tb2 = ctx.sig("tensor", lambda t_: t_.matmul(
            PS_R[:, 0:tcw], onesc[0:1, :], rowbuf[0:1, 1024:1024 + tcw],
            start=True, stop=True))
        ctx.wait("vector", "pt", tb2)
        for dt in range(8):
            t = ctx.dma("sync", xld[:, dt % 2, :],
                        src_dram[dt * 128:(dt + 1) * 128, c0:c0 + tcw])
            ctx.wait("vector", "d_sync", t)
            ctx.sig("vector", lambda v, dt=dt: v.tensor_tensor(
                out=x2s[:, :], in0=xld[:, dt % 2, :],
                in1=PS_BC[:, 0:tcw], op=OP.add))
            tvx = ctx.sig("vector", lambda v, dt=dt: v.tensor_tensor(
                out=qsc[:, 0:tcw], in0=x2s[:, :], in1=PS_R[:, 0:tcw],
                op=OP.mult))
            xn_write(dt, tcol, qsc[:, 0:tcw])
            ctx.wait("sync", "pv", ctx.tok("pv"))
        tve = ctx.tok("pv")
        ctx.wait("tensor", "pv", tve)



def build_L0(upto=99):
    import os
    upto = int(os.environ.get("L0_UPTO", upto))
    psums = [("ps_s", [128, 1024], F32),
             ("ps_s2", [128, 1024], F32),
             ("ps_o", [128, 1024], F32),
             ("ps_bc", [128, 512], F32),
             ("ps_r", [128, 512], F32)]
    nc, ctx, ar, ps, est = _prog(190000, psums)
    x0T = nc.dram_tensor("x0T", [D, 2560], F32, kind="ExternalInput").ap()
    wqT = nc.dram_tensor("wqT", [D, D], F32, kind="ExternalInput").ap()
    wkT = nc.dram_tensor("wkT", [D, D], F32, kind="ExternalInput").ap()
    wvT = nc.dram_tensor("wvT", [D, D], F32, kind="ExternalInput").ap()
    woT = nc.dram_tensor("woT", [D, D], F32, kind="ExternalInput").ap()
    qnw = nc.dram_tensor("qnw", [D], F32, kind="ExternalInput").ap()
    knw = nc.dram_tensor("knw", [D], F32, kind="ExternalInput").ap()
    wgT = nc.dram_tensor("wgT", [D, 3072], BF16, kind="ExternalInput").ap()
    wuT = nc.dram_tensor("wuT", [D, 3072], BF16, kind="ExternalInput").ap()
    wdT = nc.dram_tensor("wdT", [3072, D], BF16, kind="ExternalInput").ap()
    wmask = nc.dram_tensor("wmask", [16, 128, 640], F32, kind="ExternalInput").ap()
    vone = nc.dram_tensor("vone", [2560], F32, kind="ExternalInput").ap()
    x1T = nc.dram_tensor("x1T", [D, TC], F32, kind="ExternalOutput").ap()
    xn1T = nc.dram_tensor("xn1T", [D, TC], BF16, kind="ExternalOutput").ap()

    PS_S, PS_S2, PS_O, PS_BC, PS_R = (ps[k] for k in
                                      ("ps_s", "ps_s2", "ps_o", "ps_bc", "ps_r"))

    with est:
        woT_b = ar.alloc([128, 8, 1024], BF16)
        qnw_t = ar.alloc([128, 8], F32)
        knw_t = ar.alloc([128, 8], F32)
        onesc = ar.alloc([128, 128], F32)
        rowbuf = ar.alloc([128, 2048], F32)
        wscr = ar.alloc([128, 1024], F32)
        bcsb = ar.alloc([128, 512], F32)
        qsc = ar.alloc([128, 512], F32)
        xld = ar.alloc([128, 2, 512], F32)
        x2s = ar.alloc([128, 512], F32)

        def cast_w(dst, src, cols=1024):
            for ki in range(8):
                t = ctx.dma("sync", wscr[:, 0:cols], src[ki * 128:(ki + 1) * 128, :])
                ctx.wait("vector", "d_sync", t)
                tv = ctx.sig("vector", lambda v, ki=ki: v.tensor_copy(
                    out=dst[:, ki, :], in_=wscr[:, 0:cols]))
                ctx.wait("sync", "pv", tv)

        ctx.do("vector", lambda v: v.memset(onesc[:, :], 1.0))
        ctx.dma("sync", qnw_t[:, :], qnw.rearrange("(dt p) -> p dt", p=128), allow_slow_non_contiguous=True)
        ctx.dma("sync", knw_t[:, :], knw.rearrange("(dt p) -> p dt", p=128), allow_slow_non_contiguous=True)
        cast_w(woT_b, woT)
        ctx.barrier()

        def ln_fm(src_dram, src_c0, xn_write, ntc, eps, tcw=512):
            emit_ln_fm(ctx, PS_R, PS_BC, onesc, rowbuf, xld, x2s, qsc,
                       src_dram, src_c0, xn_write, ntc, eps, tcw)

        for half in range(2):
            ar.mark()
            sp0 = 1024 * half
            own0 = 512 + 1024 * half
            xn = ar.alloc([128, 8, 1536], BF16)
            attno = ar.alloc([128, 8, 1024], BF16)

            ar.mark()

            def xnw(dt, tcol, src_ap):
                ctx.sig("vector", lambda v, dt=dt, tcol=tcol: v.tensor_copy(
                    out=xn[:, dt, tcol * 512:(tcol + 1) * 512], in_=src_ap))
            if upto >= 1:
                ln_fm(x0T, sp0, xnw, 3, 1e-5)
            ar.release()
            ctx.barrier()

            ar.mark()
            Qb = ar.alloc([128, 8, 1024], BF16)
            Kb = ar.alloc([128, 8, 1536], BF16)
            Vb = ar.alloc([128, 12, 16, 65], BF16)
            qraw = ar.alloc([128, 8, 512], BF16)
            wtmp = ar.alloc([128, 8, 1024], BF16)
            vcol = ar.alloc([128, 12], F32)

            if os.environ.get("L0_NOVONE", "") != "1":
                tvo = ctx.dma("sync", vcol[:, :],
                              vone[sp0:sp0 + 1536].rearrange("(t p) -> p t", p=128),
                              allow_slow_non_contiguous=True)
                ctx.wait("vector", "d_sync", tvo)
                ctx.sig("vector", lambda v: v.tensor_copy(
                    out=Vb[:, :, :, 64:65],
                    in_=vcol[:, :, None, None].broadcast_to([128, 12, 16, 1])))

            PH2_ON = upto >= 2
            PROJLVL = int(os.environ.get("L0_PROJLVL", "3"))

            def proj(dst_fm, wsrc, ntc, nw_col, src_off):
                if not PH2_ON:
                    return
                cast_w(wtmp, wsrc)
                ctx.wait("tensor", "pv", ctx.tok("pv"))
                for tcol in range(ntc):
                    for dqt in range(8):
                        pj = PS_S if dqt % 2 == 0 else PS_S2
                        tmm = ctx.sig("tensor", lambda t, dqt=dqt, tcol=tcol,
                                      pj=pj: [
                            t.matmul(pj[:, 0:512],
                                     wtmp[:, ki, dqt * 128:(dqt + 1) * 128],
                                     xn[:, ki, src_off + tcol * 512:
                                        src_off + (tcol + 1) * 512],
                                     start=(ki == 0), stop=(ki == 7))
                            for ki in range(8)][-1])
                        ctx.wait("vector", "pt", tmm)
                        tvq = ctx.sig("vector", lambda v, dqt=dqt, pj=pj:
                                      v.tensor_copy(out=qraw[:, dqt, :],
                                                    in_=pj[:, 0:512]))
                        ctx.wait("tensor", "pv", tvq)
                        if PROJLVL >= 2:
                            if True:
                                ctx.wait("scalar", "pv", tvq)
                                tsq = ctx.sig("scalar", lambda s, dqt=dqt:
                                              s.activation(out=x2s[:, :],
                                                           in_=qraw[:, dqt, :],
                                                           func=AF.Square))
                            else:
                                ctx.wait("scalar", "pt", tmm)
                                tsq = ctx.sig("scalar", lambda s, pj=pj:
                                              s.activation(out=x2s[:, :],
                                                           in_=pj[:, 0:512],
                                                           func=AF.Square))
                            ctx.wait("tensor", "ps", tsq)
                            tm2 = ctx.sig("tensor", lambda t, dqt=dqt: t.matmul(
                                PS_R[0:1, 0:512], onesc[:, 0:1], x2s[:, :],
                                start=(dqt == 0), stop=(dqt == 7)))
                            ctx.wait("scalar", "pt", tm2)
                    if PROJLVL < 3:
                        continue
                    ctx.wait("vector", "pt", ctx.tok("pt"))
                    tvr = ctx.sig("vector", lambda v: v.tensor_scalar(
                        out=rowbuf[0:1, 0:512], in0=PS_R[0:1, 0:512],
                        scalar1=1.0 / D, scalar2=1e-6, op0=OP.mult, op1=OP.add))
                    ctx.wait("scalar", "pv", tvr)
                    ts2 = ctx.sig("scalar", lambda s: s.activation(
                        out=rowbuf[0:1, 512:1024], in_=rowbuf[0:1, 0:512],
                        func=AF.Sqrt))
                    ctx.wait("vector", "ps", ts2)
                    tv3 = ctx.sig("vector", lambda v: v.reciprocal(
                        out=rowbuf[0:1, 0:512], in_=rowbuf[0:1, 512:1024]))
                    ctx.wait("tensor", "pv", tv3)
                    tb = ctx.sig("tensor", lambda t: t.matmul(
                        PS_BC[:, 0:512], onesc[0:1, :], rowbuf[0:1, 0:512],
                        start=True, stop=True))
                    ctx.wait("vector", "pt", tb)
                    tvb = ctx.sig("vector", lambda v: v.tensor_copy(
                        out=bcsb[:, :], in_=PS_BC[:, 0:512]))
                    for dqt in range(8):
                        ctx.sig("vector", lambda v, dqt=dqt, tcol=tcol:
                                v.scalar_tensor_tensor(
                                    out=dst_fm[:, dqt, tcol * 512:(tcol + 1) * 512],
                                    in0=qraw[:, dqt, :],
                                    scalar=nw_col[:, dqt:dqt + 1],
                                    in1=bcsb[:, :], op0=OP.mult, op1=OP.mult))
                    tve = ctx.tok("pv")
                    ctx.wait("tensor", "pv", tve)
                    ctx.wait("scalar", "pv", tve)

            proj(Qb, wqT, 2, qnw_t, own0 - sp0)
            proj(Kb, wkT, 3, knw_t, 0)

            cast_w(wtmp, wvT)
            ctx.wait("tensor", "pv", ctx.tok("pv"))
            for tt in range(12 if upto >= 3 else 0):
                pj = PS_S if tt % 2 == 0 else PS_S2
                tmm = ctx.sig("tensor", lambda t, tt=tt, pj=pj: [
                    t.matmul(pj[:, nh * 512:(nh + 1) * 512],
                             xn[:, ki, tt * 128:(tt + 1) * 128],
                             wtmp[:, ki, nh * 512:(nh + 1) * 512],
                             start=(ki == 0), stop=(ki == 7))
                    for nh in range(2) for ki in range(8)][-1])
                ctx.wait("vector", "pt", tmm)
                tv = ctx.sig("vector", lambda v, tt=tt, pj=pj: v.tensor_copy(
                    out=Vb[:, tt, :, 0:64],
                    in_=pj[:, 0:1024].rearrange("p (h dv) -> p h dv", dv=64)))
                ctx.wait("tensor", "pv", tv)
            ctx.barrier()

            esb = ar.alloc([128, 640], F32)
            ptb = ar.alloc([128, 640], BF16)
            wm = ar.alloc([128, 640], F32)
            last_pm = 0
            for h in range(16 if upto >= 4 else 0):
                hp = (h % 2) * 64
                hd = h // 2
                ctx.wait("sync", "pv", last_pm)
                twm = ctx.dma("sync", wm[:, :], wmask[h])
                tz = ctx.sig("vector", lambda v: v.memset(PS_O[:, :], 0.0))
                ctx.wait("tensor", "pv", tz)
                lastpv = {0: None, 1: None}
                for KB in range(12):
                    qs = max(128 * KB, 512)
                    qe = min(128 * KB + 640, 1536)
                    if qe <= qs:
                        continue
                    w = qe - qs
                    joff = qs - 128 * KB
                    pj = PS_S if KB % 2 == 0 else PS_S2

                    def mms(t, KB=KB, qs=qs, w=w, pj=pj, hp=hp, hd=hd):
                        ins = None
                        for c0 in range(0, w, 512):
                            cw = min(512, w - c0)
                            ins = t.matmul(
                                pj[:, c0:c0 + cw],
                                Kb[hp:hp + 64, hd, KB * 128:(KB + 1) * 128],
                                Qb[hp:hp + 64, hd,
                                   qs - 512 + c0:qs - 512 + c0 + cw],
                                start=True, stop=True)
                        return ins
                    tmm = ctx.sig("tensor", mms)
                    ctx.wait("scalar", "pt", tmm)
                    tex = ctx.sig("scalar", lambda s, w=w, pj=pj:
                                  s.activation(out=esb[:, 0:w],
                                               in_=pj[:, 0:w], func=AF.Exp,
                                               scale=0.125))
                    ctx.wait("vector", "ps", tex)
                    ctx.wait("vector", "d_sync", twm)
                    tpm = ctx.sig("vector", lambda v, w=w, joff=joff:
                                  v.tensor_tensor(
                                      out=ptb[:, 0:w], in0=esb[:, 0:w],
                                      in1=wm[:, joff:joff + w], op=OP.mult))
                    last_pm = tpm
                    ctx.wait("tensor", "pv", tpm)
                    for Qb_i in range(2):
                        b0 = 512 + Qb_i * 512
                        a = max(qs, b0)
                        bq = min(qe, b0 + 512)
                        if bq <= a:
                            continue
                        tok = ctx.sig("tensor", lambda t, KB=KB, h=h, a=a, bq=bq,
                                      qs=qs, Qb_i=Qb_i, b0=b0: t.matmul(
                                          PS_O[0:65, Qb_i * 512 + (a - b0):
                                               Qb_i * 512 + (bq - b0)],
                                          Vb[:, KB, h, :],
                                          ptb[:, a - qs:bq - qs],
                                          start=False, stop=False,
                                          skip_group_check=True))
                        lastpv[Qb_i] = tok
                    ctx.wait("vector", "pt", ctx.tok("pt"))
                    ctx.wait("tensor", "ps", tex)
                    ctx.wait("scalar", "pv", tpm)
                for Qb_i in range(2):
                    ctx.wait("vector", "pt", lastpv[Qb_i])
                    tv1 = ctx.sig("vector", lambda v, Qb_i=Qb_i: v.reciprocal(
                        out=rowbuf[0:1, 0:512],
                        in_=PS_O[64:65, Qb_i * 512:(Qb_i + 1) * 512]))
                    ctx.wait("tensor", "pv", tv1)
                    tb = ctx.sig("tensor", lambda t: t.matmul(
                        PS_BC[0:64, 0:512], onesc[0:1, 0:64],
                        rowbuf[0:1, 0:512], start=True, stop=True))
                    ctx.wait("vector", "pt", tb)
                    tv2 = ctx.sig("vector", lambda v: v.tensor_copy(
                        out=bcsb[0:64, :], in_=PS_BC[0:64, 0:512]))
                    tv3 = ctx.sig("vector", lambda v, hp=hp, hd=hd, Qb_i=Qb_i:
                                  v.tensor_tensor(
                                      out=attno[hp:hp + 64, hd,
                                                Qb_i * 512:(Qb_i + 1) * 512],
                                      in0=PS_O[0:64,
                                               Qb_i * 512:(Qb_i + 1) * 512],
                                      in1=bcsb[0:64, :], op=OP.mult))
                    ctx.wait("tensor", "pv", tv3)
            ar.release()
            ctx.barrier()

            ar.mark()
            hte = ar.alloc([128, 24, 512], BF16)
            wguc = ar.alloc([128, 2, 8, 384], BF16)
            wdb = ar.alloc([128, 24, 1024], BF16)
            xres = ar.alloc([128, 512], F32)
            x1ev = ar.alloc([128, 512], F32)

            for kf in range(24):
                ctx.dma("sync", wdb[:, kf, :], wdT[kf * 128:(kf + 1) * 128, :])
            twd_done = ctx.tok("d_sync")

            for tcol in range(2 if upto >= 5 else 0):
                for fc in range(8):
                    f0 = fc * 384
                    tldg = ctx.dma("sync", wguc[:, 0, :, :],
                                   wgT.rearrange("(kt p) f -> p kt f", p=128)
                                   [:, :, f0:f0 + 384])
                    tldu = ctx.dma("sync", wguc[:, 1, :, :],
                                   wuT.rearrange("(kt p) f -> p kt f", p=128)
                                   [:, :, f0:f0 + 384])
                    ctx.wait("tensor", "d_sync", tldu)
                    for sub in range(3):
                        ft = fc * 3 + sub
                        tg = ctx.sig("tensor", lambda t, sub=sub, tcol=tcol: [
                            t.matmul(PS_S[:, 0:512],
                                     wguc[:, 0, ki, sub * 128:(sub + 1) * 128],
                                     xn[:, ki, 512 + tcol * 512:
                                        1024 + tcol * 512],
                                     start=(ki == 0), stop=(ki == 7))
                            for ki in range(8)][-1])
                        tu = ctx.sig("tensor", lambda t, sub=sub, tcol=tcol: [
                            t.matmul(PS_S2[:, 0:512],
                                     wguc[:, 1, ki, sub * 128:(sub + 1) * 128],
                                     xn[:, ki, 512 + tcol * 512:
                                        1024 + tcol * 512],
                                     start=(ki == 0), stop=(ki == 7))
                            for ki in range(8)][-1])
                        ctx.wait("scalar", "pt", tg)
                        tsg = ctx.sig("scalar", lambda s: s.activation(
                            out=qsc[:, 0:512], in_=PS_S[:, 0:512], func=AF.Silu))
                        ctx.wait("vector", "pt", tu)
                        ctx.wait("vector", "ps", tsg)
                        th = ctx.sig("vector", lambda v, ft=ft: v.tensor_tensor(
                            out=hte[:, ft, :], in0=qsc[:, 0:512],
                            in1=PS_S2[:, 0:512], op=OP.mult))
                        ctx.wait("tensor", "pv", th)
                    ctx.wait("sync", "pv", ctx.tok("pv"))
                ctx.wait("tensor", "d_sync", twd_done)
                for dot in range(8):
                    pj = PS_S if dot % 2 == 0 else PS_S2

                    def mmout(t, dot=dot, tcol=tcol, pj=pj):
                        ins = None
                        for ki in range(8):
                            ins = t.matmul(pj[:, 0:512],
                                           woT_b[:, ki, dot * 128:(dot + 1) * 128],
                                           attno[:, ki,
                                                 tcol * 512:(tcol + 1) * 512],
                                           start=(ki == 0), stop=False)
                        for kf in range(24):
                            ins = t.matmul(pj[:, 0:512],
                                           wdb[:, kf, dot * 128:(dot + 1) * 128],
                                           hte[:, kf, :],
                                           start=False, stop=(kf == 23))
                        return ins
                    tmm = ctx.sig("tensor", mmout)
                    tx = ctx.dma("sync", xres[:, :],
                                 x0T[dot * 128:(dot + 1) * 128,
                                     own0 + tcol * 512:own0 + (tcol + 1) * 512])
                    ctx.wait("vector", "pt", tmm)
                    ctx.wait("vector", "d_sync", tx)
                    tv = ctx.sig("vector", lambda v, pj=pj: v.tensor_tensor(
                        out=x1ev[:, :], in0=pj[:, 0:512], in1=xres[:, :],
                        op=OP.add))
                    ctx.wait("tensor", "pv", tv)
                    ctx.wait("sync", "pv", tv)
                    ctx.wait("gpsimd", "pv", tv)
                    tst = ctx.dma("gpsimd",
                                  x1T[dot * 128:(dot + 1) * 128,
                                      1024 * half + tcol * 512:
                                      1024 * half + (tcol + 1) * 512],
                                  x1ev[:, :])
                    ctx.wait("vector", "d_gpsimd", tst)
            ar.release()
            ar.release()
            ctx.barrier()

        def xn1w(dt, tcol, src_ap):
            tvx = ctx.sig("vector", lambda v, dt=dt, tcol=tcol: v.tensor_copy(
                out=x2s[:, :].bitcast(BF16)[:, 0:512], in_=src_ap))
            ctx.wait("gpsimd", "pv", tvx)
            tst = ctx.dma("gpsimd",
                          xn1T[dt * 128:(dt + 1) * 128,
                               tcol * 512:(tcol + 1) * 512],
                          x2s[:, :].bitcast(BF16)[:, 0:512])
            ctx.wait("vector", "d_gpsimd", tst)

        if upto >= 6:
            ln_fm(x1T, 0, xn1w, 4, 1e-5)
        ctx.barrier()
        ctx.wait("gpsimd", "d_gpsimd", ctx.tok("d_gpsimd"))
        with nc.Block() as block:
            ctx.replay(block)
    return nc


def l0_in_maps(inp, prep):
    x = np.asarray(inp["L0_attn_x"])
    wmaskc = l0_wmask()
    maps = []
    for c in range(NCORE):
        b, q = c // 4, c % 4
        t0 = q * TC
        xsl = np.zeros((2560, D), np.float32)
        lo = max(0, t0 - HALO)
        xsl[HALO - (t0 - lo):HALO] = x[b, lo:t0]
        xsl[HALO:] = x[b, t0:t0 + TC]
        von = np.ones((2560,), np.float32)
        if t0 == 0:
            von[:HALO] = 0.0
        m = {
            "x0T": np.ascontiguousarray(xsl.T),
            "wqT": np.ascontiguousarray(np.asarray(inp["L0_attn_wq"]).T),
            "wkT": np.ascontiguousarray(np.asarray(inp["L0_attn_wk"]).T),
            "wvT": np.ascontiguousarray(np.asarray(inp["L0_attn_wv"]).T),
            "woT": np.ascontiguousarray(np.asarray(inp["L0_attn_wo"]).T),
            "qnw": np.ascontiguousarray(np.asarray(inp["L0_attn_q_norm_w"])),
            "knw": np.ascontiguousarray(np.asarray(inp["L0_attn_k_norm_w"])),
            "wgT": np.asarray(prep["wg0T"]),
            "wuT": np.asarray(prep["wu0T"]),
            "wdT": np.asarray(prep["wd0T"]),
            "wmask": wmaskc,
            "vone": von,
        }
        maps.append(m)
    return maps



def l1_tables():
    t = np.arange(T, dtype=np.float64)[:, None]
    i = np.arange(32, dtype=np.float64)[None, :]
    theta = 10000.0 ** (-2.0 * i / 64.0)
    fr = t * theta
    rope = np.concatenate([np.cos(fr), np.sin(fr)], -1)
    C = rope[:, 0::2]
    S = rope[:, 1::2]
    j64 = np.arange(128) % 64
    ct = np.where(j64[:, None] < 64, C[:, j64 % 32].T, 0.0)
    st_sign = np.where(j64 < 32, -1.0, 1.0)
    st = st_sign[:, None] * S[:, j64 % 32].T
    m = np.arange(128) % 64
    ju = m // 2
    ut = -C[:, ju].T
    wt_sign = np.where(m % 2 == 0, -1.0, 1.0)
    wt = wt_sign[:, None] * S[:, ju].T
    return (ct.astype(np.float32), st.astype(np.float32),
            ut.astype(np.float32), wt.astype(np.float32),
            C.astype(np.float32), S.astype(np.float32))


def _perm_sigma():
    ev = np.arange(0, 64, 2)
    od = np.arange(1, 64, 2)
    sA = np.concatenate([ev, od])
    sB = np.concatenate([od, ev])
    gA = np.concatenate([h * 64 + sA for h in range(16)])
    gB = np.concatenate([h * 64 + sB for h in range(16)])
    p1 = np.zeros(1024, np.int64)
    p2 = np.zeros(1024, np.int64)
    for h in range(16):
        for j in range(32):
            p1[h * 64 + 2 * j] = h * 64 + j
            p1[h * 64 + 2 * j + 1] = h * 64 + 32 + j
            p2[h * 64 + 2 * j] = h * 64 + 32 + j
            p2[h * 64 + 2 * j + 1] = h * 64 + j
    return gA, gB, p1, p2


def build_L1():
    psums = [("ps_a", [128, 1024], F32),
             ("ps_b", [128, 1024], F32),
             ("ps_c", [128, 1024], F32),
             ("ps_d", [128, 1024], F32)]
    nc, ctx, ar, ps, est = _prog(190000, psums)
    PS_A, PS_B, PS_C, PS_D = (ps[k] for k in ("ps_a", "ps_b", "ps_c", "ps_d"))
    x1T = nc.dram_tensor("x1T", [D, TC], F32, kind="ExternalInput").ap()
    xnfT = nc.dram_tensor("xnfT", [D, T], BF16, kind="ExternalInput").ap()
    wqTa = nc.dram_tensor("wqTa", [D, D], F32, kind="ExternalInput").ap()
    wqTb = nc.dram_tensor("wqTb", [D, D], F32, kind="ExternalInput").ap()
    wkTa = nc.dram_tensor("wkTa", [D, D], F32, kind="ExternalInput").ap()
    wkTb = nc.dram_tensor("wkTb", [D, D], F32, kind="ExternalInput").ap()
    wvT = nc.dram_tensor("wvT", [D, D], F32, kind="ExternalInput").ap()
    m1T = nc.dram_tensor("m1T", [D, D], F32, kind="ExternalInput").ap()
    m2T = nc.dram_tensor("m2T", [D, D], F32, kind="ExternalInput").ap()
    bqa = nc.dram_tensor("bqa", [D], F32, kind="ExternalInput").ap()
    bqb = nc.dram_tensor("bqb", [D], F32, kind="ExternalInput").ap()
    bka = nc.dram_tensor("bka", [D], F32, kind="ExternalInput").ap()
    bkb = nc.dram_tensor("bkb", [D], F32, kind="ExternalInput").ap()
    bv = nc.dram_tensor("bv", [D], F32, kind="ExternalInput").ap()
    bo = nc.dram_tensor("bo", [D], F32, kind="ExternalInput").ap()
    ctabK = nc.dram_tensor("ctabK", [128, T], F32, kind="ExternalInput").ap()
    stabK = nc.dram_tensor("stabK", [128, T], F32, kind="ExternalInput").ap()
    ctabQ = nc.dram_tensor("ctabQ", [128, TC], F32, kind="ExternalInput").ap()
    stabQ = nc.dram_tensor("stabQ", [128, TC], F32, kind="ExternalInput").ap()
    utab = nc.dram_tensor("utab", [128, TC], F32, kind="ExternalInput").ap()
    wtab = nc.dram_tensor("wtab", [128, TC], F32, kind="ExternalInput").ap()
    cV = nc.dram_tensor("cV", [T, 32], F32, kind="ExternalInput").ap()
    sV = nc.dram_tensor("sV", [T, 32], F32, kind="ExternalInput").ap()
    selT = nc.dram_tensor("selT", [D, 32], F32, kind="ExternalInput").ap()
    keysb = nc.dram_tensor("keysb", [32, D, 512], BF16, kind="ExternalInput").ap()
    valsb = nc.dram_tensor("valsb", [32, 512, D], BF16, kind="ExternalInput").ap()
    x2T = nc.dram_tensor("x2T", [D, TC], F32, kind="ExternalOutput").ap()
    xn2T = nc.dram_tensor("xn2T", [D, TC], BF16, kind="ExternalOutput").ap()
    xn2F = nc.dram_tensor("xn2F", [D, TC], F32, kind="ExternalOutput").ap()
    eye = nc.dram_tensor("eye", [128, 128], F32, kind="ExternalInput").ap()
    KrD = nc.dram_tensor("KrD", [16, 64, T], BF16, kind="Internal").ap()
    VrD = nc.dram_tensor("VrD", [16, 64, 128, 65], BF16, kind="Internal").ap()
    gTD = nc.dram_tensor("gTD", [32, TC], F32, kind="Internal").ap()

    with est:
        onesc = ar.alloc([128, 128], F32)
        rowbuf = ar.alloc([128, 2048], F32)
        wscr = ar.alloc([128, 1024], F32)
        bcsb = ar.alloc([128, 512], F32)
        x2s = ar.alloc([128, 512], F32)
        xld = ar.alloc([128, 2, 512], F32)
        colb = ar.alloc([128, 48], F32)
        bvrow = ar.alloc([128, 1024], F32)
        qsc = ar.alloc([128, 512], F32)
        xn_own = ar.alloc([128, 8, 2048], BF16)

        ctx.do("vector", lambda v: v.memset(onesc[:, :], 1.0))
        tbv = ctx.dma("sync", bvrow[0:1, :], bv[None, :])
        COLS = {}
        for idx, (nm, ap_) in enumerate([("bqa", bqa), ("bqb", bqb),
                                         ("bka", bka), ("bkb", bkb),
                                         ("bo", bo)]):
            COLS[nm] = idx * 8
            ctx.dma("sync", colb[:, idx * 8:idx * 8 + 8],
                    ap_.rearrange("(dt p) -> p dt", p=128),
                    allow_slow_non_contiguous=True)
        for dt in range(8):
            for cc in range(4):
                ctx.dma("sync", xn_own[:, dt, cc * 512:(cc + 1) * 512],
                        xnfT[dt * 128:(dt + 1) * 128, cc * 512:(cc + 1) * 512])
        ctx.barrier()

        def cast_w(dst, src, cols=1024, ktiles=8):
            for ki in range(ktiles):
                t = ctx.dma("sync", wscr[:, 0:cols], src[ki * 128:(ki + 1) * 128, :])
                ctx.wait("vector", "d_sync", t)
                tv = ctx.sig("vector", lambda v, ki=ki: v.tensor_copy(
                    out=dst[:, ki, :], in_=wscr[:, 0:cols]))
                ctx.wait("sync", "pv", tv)

        ar.mark()
        wa = ar.alloc([128, 8, 1024], BF16)
        wb = ar.alloc([128, 8, 1024], BF16)
        wv_b = ar.alloc([128, 8, 1024], BF16)
        ka = ar.alloc([128, 8, 1024], BF16)
        kb = ar.alloc([128, 8, 1024], BF16)
        kr = ka
        ctt = ar.alloc([128, 1024], F32)
        stt_ = ar.alloc([128, 1024], F32)
        vtm = ar.alloc([128, 16, 65], BF16)
        vrp = ar.alloc([128, 16, 65], BF16)
        cvt = ar.alloc([128, 32], F32)
        svt = ar.alloc([128, 32], F32)
        rsc = ar.alloc([128, 2, 512], F32)
        xnf = ar.alloc([128, 8, 2, 512], BF16)

        cast_w(wa, wkTa)
        cast_w(wb, wkTb)
        cast_w(wv_b, wvT)
        ctx.barrier()

        def load_xnf(c0, buf):
            toks = []
            for ki in range(8):
                toks.append(ctx.dma("sync", xnf[:, ki, buf, :],
                                    xnfT[ki * 128:(ki + 1) * 128, c0:c0 + 512]))
            return toks[-1]

        for ch in range(8):
            c0 = ch * 1024
            ctx.wait("sync", "pv", ctx.tok("pv"))
            tc1 = load_xnf(c0, 0)
            tc2 = load_xnf(c0 + 512, 1)
            tct = ctx.dma("sync", ctt[:, :], ctabK[:, c0:c0 + 1024])
            tst = ctx.dma("sync", stt_[:, :], stabK[:, c0:c0 + 1024])
            ctx.wait("tensor", "d_sync", tc2)
            for dt in range(8):
                for half_, (wsb, dst, bcol) in enumerate(
                        [(wa, ka, COLS["bka"]), (wb, kb, COLS["bkb"])]):
                    pj = PS_A if half_ == 0 else PS_B
                    for cc in range(2):
                        tmm = ctx.sig("tensor", lambda t, dt=dt, cc=cc, wsb=wsb,
                                      pj=pj: [
                            t.matmul(pj[:, cc * 512:(cc + 1) * 512],
                                     wsb[:, ki, dt * 128:(dt + 1) * 128],
                                     xnf[:, ki, cc, :],
                                     start=(ki == 0), stop=(ki == 7))
                            for ki in range(8)][-1])
                    ctx.wait("vector", "pt", ctx.tok("pt"))
                    tv = ctx.sig("vector", lambda v, dt=dt, dst=dst, pj=pj,
                                 bcol=bcol: v.tensor_scalar(
                                     out=dst[:, dt, :], in0=pj[:, 0:1024],
                                     scalar1=colb[:, bcol + dt:bcol + dt + 1],
                                     scalar2=None, op0=OP.add))
                    ctx.wait("tensor", "pv", tv)
                ctx.wait("vector", "d_sync", tst)
                for cc in range(2):
                    ctx.sig("vector", lambda v, dt=dt, cc=cc: v.tensor_tensor(
                        out=rsc[:, 0, :], in0=ka[:, dt, cc * 512:(cc + 1) * 512],
                        in1=ctt[:, cc * 512:(cc + 1) * 512], op=OP.mult))
                    ctx.sig("vector", lambda v, dt=dt, cc=cc: v.tensor_tensor(
                        out=rsc[:, 1, :], in0=kb[:, dt, cc * 512:(cc + 1) * 512],
                        in1=stt_[:, cc * 512:(cc + 1) * 512], op=OP.mult))
                    tvk = ctx.sig("vector", lambda v, dt=dt, cc=cc: v.tensor_tensor(
                        out=kr[:, dt, cc * 512:(cc + 1) * 512], in0=rsc[:, 0, :],
                        in1=rsc[:, 1, :], op=OP.add))
                ctx.wait("gpsimd", "pv", ctx.tok("pv"))
                tk1 = ctx.dma("gpsimd", KrD[2 * dt, :, c0:c0 + 1024],
                              kr[0:64, dt, :])
                tk2 = ctx.dma("gpsimd", KrD[2 * dt + 1, :, c0:c0 + 1024],
                              kr[64:128, dt, :])
                ctx.wait("vector", "d_gpsimd", tk2)
            for tt in range(8):
                t0 = c0 + tt * 128
                ctx.wait("sync", "pv", ctx.tok("pv"))
                tcv = ctx.dma("sync", cvt[:, :], cV[t0:t0 + 128, :])
                tsv = ctx.dma("sync", svt[:, :], sV[t0:t0 + 128, :])
                pj = PS_A if tt % 2 == 0 else PS_B
                def vmm(t, tt=tt, pj=pj):
                    ins = None
                    for nh in range(2):
                        for ki in range(8):
                            ins = t.matmul(
                                pj[:, nh * 512:(nh + 1) * 512],
                                xnf[:, ki, tt // 4, (tt % 4) * 128:
                                    (tt % 4) * 128 + 128],
                                wv_b[:, ki, nh * 512:(nh + 1) * 512],
                                start=(ki == 0), stop=False)
                        ins = t.matmul(pj[:, nh * 512:(nh + 1) * 512],
                                       onesc[0:1, 0:128],
                                       bvrow[0:1, nh * 512:(nh + 1) * 512],
                                       start=False, stop=True)
                    return ins
                tmm = ctx.sig("tensor", vmm)
                ctx.wait("vector", "pt", tmm)
                tvz = ctx.sig("vector", lambda v: v.memset(vrp[:, :, 64:65], 1.0))
                tvc = ctx.sig("vector", lambda v, pj=pj: v.tensor_copy(
                    out=vtm[:, :, 0:64],
                    in_=pj[:, 0:1024].rearrange("p (h dv) -> p h dv", dv=64)))
                ctx.wait("vector", "d_sync", tsv)
                v1 = vtm[:, :, 0:64:2]
                v2 = vtm[:, :, 1:64:2]
                cb = cvt[:, None, :].broadcast_to([128, 16, 32])
                sb = svt[:, None, :].broadcast_to([128, 16, 32])
                ctx.sig("vector", lambda v: v.tensor_tensor(
                    out=rsc[:, 0, :].rearrange("p (h j) -> p h j", j=32),
                    in0=v1, in1=cb, op=OP.mult))
                ctx.sig("vector", lambda v: v.tensor_tensor(
                    out=rsc[:, 1, :].rearrange("p (h j) -> p h j", j=32),
                    in0=v2, in1=sb, op=OP.mult))
                ctx.sig("vector", lambda v: v.tensor_tensor(
                    out=vrp[:, :, 0:32],
                    in0=rsc[:, 0, :].rearrange("p (h j) -> p h j", j=32),
                    in1=rsc[:, 1, :].rearrange("p (h j) -> p h j", j=32),
                    op=OP.subtract))
                ctx.sig("vector", lambda v: v.tensor_tensor(
                    out=rsc[:, 0, :].rearrange("p (h j) -> p h j", j=32),
                    in0=v1, in1=sb, op=OP.mult))
                ctx.sig("vector", lambda v: v.tensor_tensor(
                    out=rsc[:, 1, :].rearrange("p (h j) -> p h j", j=32),
                    in0=v2, in1=cb, op=OP.mult))
                tvr = ctx.sig("vector", lambda v: v.tensor_tensor(
                    out=vrp[:, :, 32:64],
                    in0=rsc[:, 0, :].rearrange("p (h j) -> p h j", j=32),
                    in1=rsc[:, 1, :].rearrange("p (h j) -> p h j", j=32),
                    op=OP.add))
                ctx.wait("gpsimd", "pv", tvr)
                for h in range(16):
                    tvd = ctx.dma("gpsimd", VrD[h, t0 // 128, :, :],
                                  vrp[:, h, :])
                ctx.wait("vector", "d_gpsimd", ctx.tok("d_gpsimd"))
                ctx.wait("tensor", "pv", tvc)
            ctx.wait("sync", "pt", ctx.tok("pt"))
        ar.release()
        ctx.barrier()

        Oall = ar.alloc([128, 8, 2048], BF16)
        ar.mark()
        Qr = ar.alloc([128, 8, 2048], BF16)
        ar.mark()
        wqab = ar.alloc([128, 8, 1024], BF16)
        wqbb = ar.alloc([128, 8, 1024], BF16)
        qa = ar.alloc([128, 8, 512], BF16)
        qb = ar.alloc([128, 8, 512], BF16)
        cttq = ar.alloc([128, 512], F32)
        sttq = ar.alloc([128, 512], F32)
        rscq = ar.alloc([128, 2, 512], F32)
        cast_w(wqab, wqTa)
        cast_w(wqbb, wqTb)
        ctx.barrier()
        for tcol in range(4):
            tct = ctx.dma("sync", cttq[:, :], ctabQ[:, tcol * 512:(tcol + 1) * 512])
            tst = ctx.dma("sync", sttq[:, :], stabQ[:, tcol * 512:(tcol + 1) * 512])
            for dt in range(8):
                for half_, (wsb, dst, bcol) in enumerate(
                        [(wqab, qa, COLS["bqa"]), (wqbb, qb, COLS["bqb"])]):
                    pj = PS_A if half_ == 0 else PS_B
                    tmm = ctx.sig("tensor", lambda t, dt=dt, tcol=tcol, wsb=wsb,
                                  pj=pj: [
                        t.matmul(pj[:, 0:512],
                                 wsb[:, ki, dt * 128:(dt + 1) * 128],
                                 xn_own[:, ki, tcol * 512:(tcol + 1) * 512],
                                 start=(ki == 0), stop=(ki == 7))
                        for ki in range(8)][-1])
                    ctx.wait("vector", "pt", tmm)
                    tv = ctx.sig("vector", lambda v, dt=dt, dst=dst, pj=pj,
                                 bcol=bcol: v.tensor_scalar(
                                     out=dst[:, dt, :], in0=pj[:, 0:512],
                                     scalar1=colb[:, bcol + dt:bcol + dt + 1],
                                     scalar2=None, op0=OP.add))
                    ctx.wait("tensor", "pv", tv)
                ctx.wait("vector", "d_sync", tst)
                ctx.sig("vector", lambda v, dt=dt: v.tensor_tensor(
                    out=rscq[:, 0, :], in0=qa[:, dt, :], in1=cttq[:, :],
                    op=OP.mult))
                ctx.sig("vector", lambda v, dt=dt: v.tensor_tensor(
                    out=rscq[:, 1, :], in0=qb[:, dt, :], in1=sttq[:, :],
                    op=OP.mult))
                ctx.sig("vector", lambda v, dt=dt, tcol=tcol: v.tensor_tensor(
                    out=Qr[:, dt, tcol * 512:(tcol + 1) * 512],
                    in0=rscq[:, 0, :], in1=rscq[:, 1, :], op=OP.add))
            ctx.wait("sync", "pv", ctx.tok("pv"))
        ar.release()
        ctx.barrier()

        kst = ar.alloc([128, 2, 1024], BF16)
        vst = ar.alloc([128, 2, 8, 65], BF16)
        ptb = ar.alloc([128, 2, 512], BF16)
        for h in range(16):
            hp = (h % 2) * 64
            hd = h // 2
            for qc in range(4):
                for kc in range(8):
                    bufi = kc % 2
                    tkl = ctx.dma("sync", kst[hp:hp + 64, bufi, :],
                                  KrD[h, :, kc * 1024:(kc + 1) * 1024])
                    tvl = ctx.dma("sync", vst[:, bufi, :, :],
                                  VrD[h, kc * 8:kc * 8 + 8, :, :]
                                  .rearrange("n p c -> p n c"))
                    ctx.wait("tensor", "d_sync", tvl)
                    for kt in range(8):
                        ktg = kc * 8 + kt
                        pj = PS_A if ktg % 2 == 0 else PS_B
                        tmm = ctx.sig("tensor", lambda t, kt=kt, bufi=bufi,
                                      qc=qc, pj=pj, hp=hp, hd=hd: t.matmul(
                                          pj[:, 0:512],
                                          kst[hp:hp + 64, bufi,
                                              kt * 128:(kt + 1) * 128],
                                          Qr[hp:hp + 64, hd,
                                             qc * 512:(qc + 1) * 512],
                                          start=True, stop=True))
                        ctx.wait("scalar", "pt", tmm)
                        tex = ctx.sig("scalar", lambda s, pj=pj, bufi=bufi, kt=kt:
                                      s.activation(out=ptb[:, kt % 2, :],
                                                   in_=pj[:, 0:512],
                                                   func=AF.Exp, scale=0.125))
                        ctx.wait("tensor", "ps", tex)
                        tpv = ctx.sig("tensor", lambda t, kt=kt, bufi=bufi,
                                      ktg=ktg: t.matmul(
                                          PS_C[0:65, 0:512],
                                          vst[:, bufi, kt, :],
                                          ptb[:, kt % 2, :],
                                          start=(ktg == 0), stop=(ktg == 63)))
                        ctx.wait("scalar", "pt", tpv)
                ctx.wait("vector", "pt", ctx.tok("pt"))
                tv1 = ctx.sig("vector", lambda v: v.reciprocal(
                    out=rowbuf[0:1, 0:512], in_=PS_C[64:65, 0:512]))
                ctx.wait("tensor", "pv", tv1)
                tb = ctx.sig("tensor", lambda t: t.matmul(
                    PS_D[0:64, 0:512], onesc[0:1, 0:64], rowbuf[0:1, 0:512],
                    start=True, stop=True))
                ctx.wait("vector", "pt", tb)
                tv2 = ctx.sig("vector", lambda v: v.tensor_copy(
                    out=bcsb[0:64, :], in_=PS_D[0:64, 0:512]))
                tv3 = ctx.sig("vector", lambda v, hp=hp, hd=hd, qc=qc:
                              v.tensor_tensor(
                                  out=Oall[hp:hp + 64, hd,
                                           qc * 512:(qc + 1) * 512],
                                  in0=PS_C[0:64, 0:512],
                                  in1=bcsb[0:64, :], op=OP.mult))
                ctx.wait("tensor", "pv", tv3)
        ar.release()
        ctx.barrier()

        ar.mark()
        selb = ar.alloc([128, 8, 32], BF16)
        gf = ar.alloc([128, 16, 32], F32)
        g1 = ar.alloc([128, 64], F32)
        gmk = ar.alloc([128, 2], F32)
        eyeb = ar.alloc([128, 128], F32)
        m1b = ar.alloc([128, 8, 1024], BF16)
        m2b = ar.alloc([128, 8, 1024], BF16)
        gbrow = ar.alloc([128, 2048], F32)
        gbc = ar.alloc([128, 256], F32)
        Oub = ar.alloc([128, 8, 256], BF16)
        Owb = ar.alloc([128, 8, 256], BF16)
        uts = ar.alloc([128, 2, 256], F32)
        ksb = ar.alloc([128, 1, 8, 512], BF16)
        vsb = ar.alloc([128, 1, 4, 1024], BF16)
        sgb = ar.alloc([128, 4, 256], BF16)
        xres = ar.alloc([128, 256], F32)
        x2ev = ar.alloc([128, 256], F32)

        teye = ctx.dma("sync", eyeb[:, :], eye[:, :])
        cast_w(selb, selT, cols=32)
        cast_w(m1b, m1T)
        cast_w(m2b, m2T)
        ctx.barrier()

        for tt in range(16):
            tmm = ctx.sig("tensor", lambda t, tt=tt: [
                t.matmul(PS_A[:, 0:32], xn_own[:, ki, tt * 128:(tt + 1) * 128],
                         selb[:, ki, :], start=(ki == 0), stop=(ki == 7))
                for ki in range(8)][-1])
            ctx.wait("scalar", "pt", tmm)
            tsg = ctx.sig("scalar", lambda s, tt=tt: s.activation(
                out=gf[:, tt, :], in_=PS_A[:, 0:32], func=AF.Sigmoid))
            ctx.wait("vector", "ps", tsg)
            gp = [g1[:, 0:32], g1[:, 32:64]]
            tvc = ctx.sig("vector", lambda v, tt=tt: v.tensor_copy(
                out=gp[0], in_=gf[:, tt, :]))
            for it in range(3):
                src_, dst_ = gp[it % 2], gp[(it + 1) % 2]
                ctx.sig("vector", lambda v, src_=src_: v.reduce_max(
                    out=gmk[:, 0:1], in_=src_, axis=AX))
                ctx.sig("vector", lambda v, src_=src_: v.tensor_scalar(
                    out=x2s[:, 0:32], in0=src_, scalar1=gmk[:, 0:1],
                    scalar2=None, op0=OP.is_ge))
                ctx.sig("vector", lambda v, src_=src_, dst_=dst_:
                        v.scalar_tensor_tensor(
                            out=dst_, in0=x2s[:, 0:32], scalar=-1e30,
                            in1=src_, op0=OP.mult, op1=OP.add))
            ctx.sig("vector", lambda v: v.reduce_max(
                out=gmk[:, 1:2], in_=gp[1], axis=AX))
            ctx.sig("vector", lambda v, tt=tt: v.tensor_scalar(
                out=x2s[:, 0:32], in0=gf[:, tt, :], scalar1=gmk[:, 1:2],
                scalar2=None, op0=OP.is_ge))
            tvg = ctx.sig("vector", lambda v, tt=tt: v.tensor_tensor(
                out=gp[0], in0=x2s[:, 0:32], in1=gf[:, tt, :],
                op=OP.mult))
            ctx.wait("tensor", "pv", tvg)
            ctx.wait("tensor", "d_sync", teye)
            ttr = ctx.sig("tensor", lambda t, tt=tt: t.transpose(
                PS_B[0:32, 0:128], gp[0], eyeb[:, :]))
            ctx.wait("vector", "pt", ttr)
            tge2 = ctx.sig("vector", lambda v: v.tensor_copy(
                out=g1[0:32, :].bitcast(F32)[0:32, 0:128]
                if False else gbc[0:32, 0:128], in_=PS_B[0:32, 0:128]))
            ctx.wait("gpsimd", "pv", tge2)
            tgd = ctx.dma("gpsimd", gTD[:, tt * 128:(tt + 1) * 128],
                          gbc[0:32, 0:128])
            ctx.wait("tensor", "pv", tge2)
            ctx.wait("vector", "d_gpsimd", tgd)
        ctx.barrier()

        for tcol in range(8):
            tc0 = tcol * 256
            tu1 = ctx.dma("sync", uts[:, 0, :], utab[:, tc0:tc0 + 256])
            tu2 = ctx.dma("sync", uts[:, 1, :], wtab[:, tc0:tc0 + 256])
            ctx.wait("vector", "d_sync", tu2)
            for dt in range(8):
                ctx.sig("vector", lambda v, dt=dt: v.tensor_tensor(
                    out=Oub[:, dt, :], in0=Oall[:, dt, tc0:tc0 + 256],
                    in1=uts[:, 0, :], op=OP.mult))
                ctx.sig("vector", lambda v, dt=dt: v.tensor_tensor(
                    out=Owb[:, dt, :], in0=Oall[:, dt, tc0:tc0 + 256],
                    in1=uts[:, 1, :], op=OP.mult))
            tou = ctx.tok("pv")
            ctx.wait("tensor", "pv", tou)

            def oslot(do):
                pj = PS_C if do < 4 else PS_D
                c0 = (do % 4) * 256
                return pj, c0

            tz1 = ctx.sig("vector", lambda v: v.memset(PS_C[:, :], 0.0))
            tz2 = ctx.sig("vector", lambda v: v.memset(PS_D[:, :], 0.0))
            ctx.wait("tensor", "pv", tz2)
            for do in range(8):
                pj, c0 = oslot(do)
                ctx.sig("tensor", lambda t, do=do, pj=pj, c0=c0: [
                    t.matmul(pj[:, c0:c0 + 256],
                             m1b[:, ki, do * 128:(do + 1) * 128],
                             Oub[:, ki, :], start=False, stop=False,
                             skip_group_check=True)
                    for ki in range(8)][-1])
                ctx.sig("tensor", lambda t, do=do, pj=pj, c0=c0: [
                    t.matmul(pj[:, c0:c0 + 256],
                             m2b[:, ki, do * 128:(do + 1) * 128],
                             Owb[:, ki, :], start=False, stop=False,
                             skip_group_check=True)
                    for ki in range(8)][-1])
            for e in range(32):
                bi = 0
                if e % 8 == 0:
                    tgb = ctx.dma(
                        "sync",
                        gbrow[0:1, :].rearrange("p (e t) -> p e t", t=256),
                        gTD[None, e:e + 8, tc0:tc0 + 256])
                    ctx.wait("tensor", "d_sync", tgb)
                tkl = ctx.dma("sync", ksb[:, bi, :, :],
                              keysb[e].rearrange("(k p) f -> p k f", p=128))
                tvl = ctx.dma("sync", vsb[:, bi, :, :],
                              valsb[e].rearrange("(k p) f -> p k f", p=128))
                tgm = ctx.sig("tensor", lambda t, e=e: t.matmul(
                    PS_B[:, 0:256], onesc[0:1, :],
                    gbrow[0:1, (e % 8) * 256:(e % 8 + 1) * 256],
                    start=True, stop=True))
                ctx.wait("vector", "pt", tgm)
                tge = ctx.sig("vector", lambda v: v.tensor_copy(
                    out=gbc[:, :], in_=PS_B[:, 0:256]))
                ctx.wait("tensor", "d_sync", tkl)
                ctx.wait("tensor", "pv", tge)
                for fk in range(4):
                    spj = PS_A if fk < 2 else PS_B
                    sc0 = (fk % 2) * 512
                    tsm = ctx.sig("tensor", lambda t, e=e, bi=bi, fk=fk,
                                  tc0=tc0, spj=spj, sc0=sc0: [
                        t.matmul(spj[:, sc0:sc0 + 256],
                                 ksb[:, bi, ki, fk * 128:(fk + 1) * 128],
                                 xn_own[:, ki, tc0:tc0 + 256],
                                 start=(ki == 0), stop=(ki == 7))
                        for ki in range(8)][-1])
                    ctx.wait("vector", "pt", tsm)
                    tse = ctx.sig("vector", lambda v, fk=fk, spj=spj, sc0=sc0:
                                  v.scalar_tensor_tensor(
                                      out=sgb[:, fk, :],
                                      in0=spj[:, sc0:sc0 + 256],
                                      scalar=0.0, in1=gbc[:, :],
                                      op0=OP.max, op1=OP.mult))
                tse_all = ctx.tok("pv")
                ctx.wait("tensor", "pv", tse_all)
                ctx.wait("tensor", "d_sync", tvl)
                for do in range(8):
                    pj, c0 = oslot(do)
                    ctx.sig("tensor", lambda t, e=e, bi=bi, do=do, pj=pj,
                            c0=c0: [
                        t.matmul(pj[:, c0:c0 + 256],
                                 vsb[:, bi, fk, do * 128:(do + 1) * 128],
                                 sgb[:, fk, :], start=False, stop=False,
                                 skip_group_check=True)
                        for fk in range(4)][-1])
                tlast = ctx.tok("pt")
                ctx.wait("sync", "pt", tlast)
            for do in range(8):
                pj, c0 = oslot(do)
                tx = ctx.dma("sync", xres[:, :],
                             x1T[do * 128:(do + 1) * 128, tc0:tc0 + 256])
                ctx.wait("vector", "pt", ctx.tok("pt"))
                ctx.wait("vector", "d_sync", tx)
                tv = ctx.sig("vector", lambda v, do=do, pj=pj, c0=c0:
                             v.scalar_tensor_tensor(
                                 out=x2ev[:, :], in0=pj[:, c0:c0 + 256],
                                 scalar=colb[:, COLS["bo"] + do:
                                             COLS["bo"] + do + 1],
                                 in1=xres[:, :], op0=OP.add, op1=OP.add))
                ctx.wait("gpsimd", "pv", tv)
                ctx.wait("sync", "pv", tv)
                tst = ctx.dma("gpsimd",
                              x2T[do * 128:(do + 1) * 128, tc0:tc0 + 256],
                              x2ev[:, :])
                ctx.wait("vector", "d_gpsimd", tst)
                ctx.wait("tensor", "pv", tv)
        ar.release()
        ctx.barrier()

        qsc2 = qsc

        def xn2w(dt, tcol, src_ap):
            tvx = ctx.sig("vector", lambda v: v.tensor_copy(
                out=x2s[:, :].bitcast(BF16)[:, 0:512], in_=src_ap))
            ctx.wait("gpsimd", "pv", tvx)
            tst = ctx.dma("gpsimd",
                          xn2T[dt * 128:(dt + 1) * 128,
                               tcol * 512:(tcol + 1) * 512],
                          x2s[:, :].bitcast(BF16)[:, 0:512])
            tsf = ctx.dma("gpsimd",
                          xn2F[dt * 128:(dt + 1) * 128,
                               tcol * 512:(tcol + 1) * 512], src_ap)
            ctx.wait("vector", "d_gpsimd", tsf)

        emit_ln_fm(ctx, PS_A, PS_B, onesc, rowbuf, xld, x2s, qsc2,
                   x2T, 0, xn2w, 4, 1e-5)
        ctx.barrier()
        ctx.wait("gpsimd", "d_gpsimd", ctx.tok("d_gpsimd"))
        with nc.Block() as block:
            ctx.replay(block)
    return nc


def l1_in_maps(inp, prep, x1T_list, xn1_list):
    import ml_dtypes
    ct, st, ut, wt, Cv, Sv = l1_tables()
    gA, gB, p1, p2 = _perm_sigma()
    Wq = np.asarray(inp["L1_attn_Wq"]); Wk = np.asarray(inp["L1_attn_Wk"])
    Wv = np.asarray(inp["L1_attn_Wv"]); Wo = np.asarray(inp["L1_attn_Wo"])
    eye = np.eye(128, dtype=np.float32)
    base = {
        "wqTa": np.ascontiguousarray(Wq[gA].T), "wqTb": np.ascontiguousarray(Wq[gB].T),
        "wkTa": np.ascontiguousarray(Wk[gA].T), "wkTb": np.ascontiguousarray(Wk[gB].T),
        "wvT": np.ascontiguousarray(Wv.T),
        "m1T": np.ascontiguousarray(Wo[:, p1].T),
        "m2T": np.ascontiguousarray(Wo[:, p2].T),
        "bqa": np.ascontiguousarray(np.asarray(inp["L1_attn_bq"])[gA]),
        "bqb": np.ascontiguousarray(np.asarray(inp["L1_attn_bq"])[gB]),
        "bka": np.ascontiguousarray(np.asarray(inp["L1_attn_bk"])[gA]),
        "bkb": np.ascontiguousarray(np.asarray(inp["L1_attn_bk"])[gB]),
        "bv": np.ascontiguousarray(np.asarray(inp["L1_attn_bv"])),
        "bo": np.ascontiguousarray(np.asarray(inp["L1_attn_bo"])),
        "selT": np.ascontiguousarray(np.asarray(inp["L1_ffn_expert_sel"]).T),
        "keysb": np.asarray(prep["keys_bf"]),
        "valsb": np.asarray(prep["values_bf"]),
        "eye": eye,
    }
    maps = []
    for c in range(NCORE):
        b, q = c // 4, c % 4
        t0 = q * TC
        xnb = np.concatenate([np.asarray(xn1_list[b * 4 + j]) for j in range(4)],
                             axis=1)
        rot = np.roll(np.arange(T), -t0)
        m = dict(base)
        m["x1T"] = np.asarray(x1T_list[c])
        m["xnfT"] = np.ascontiguousarray(xnb[:, rot])
        m["ctabK"] = np.ascontiguousarray(ct[:, rot])
        m["stabK"] = np.ascontiguousarray(st[:, rot])
        m["cV"] = np.ascontiguousarray(Cv[rot])
        m["sV"] = np.ascontiguousarray(Sv[rot])
        m["ctabQ"] = np.ascontiguousarray(ct[:, t0:t0 + TC])
        m["stabQ"] = np.ascontiguousarray(st[:, t0:t0 + TC])
        m["utab"] = np.ascontiguousarray(ut[:, t0:t0 + TC])
        m["wtab"] = np.ascontiguousarray(wt[:, t0:t0 + TC])
        maps.append(m)
    return maps



def l3_tables():
    t = np.arange(T, dtype=np.float64)[:, None]
    j = np.arange(64, dtype=np.float64)[None, :]
    inv = 1.0 / 10000.0 ** (j / 64.0)
    ang = t * inv
    cos = np.cos(ang)
    sin = np.sin(ang)
    r = np.arange(128)
    jj = r % 64
    u3 = np.where(r[:, None] < 64, cos[:, jj].T, sin[:, jj].T)
    v3 = np.where(r[:, None] < 64, -sin[:, jj].T, cos[:, jj].T)
    return u3.astype(np.float32), v3.astype(np.float32)


def l3_cmask():
    out = np.zeros((5, 128, 512), np.float32)
    for m in range(4):
        k = 128 * m + np.arange(128)[:, None]
        q = np.arange(512)[None, :]
        out[m] = np.where(k <= q, 0.0, -1e30)
    out[4] = -1e30
    return out


def l3_grid():
    i = np.arange(128)[:, None]
    j = np.arange(128)[None, :]
    g = np.where(i > j, (i - j - 1).astype(np.float64), 1e30)
    return g.astype(np.float32)


def build_L3():
    psums = [("ps_a", [128, 1024], F32),
             ("ps_b", [128, 1024], F32),
             ("ps_c", [128, 1024], F32),
             ("ps_d", [128, 1024], F32)]
    nc, ctx, ar, ps, est = _prog(190000, psums)
    PS_A, PS_B, PS_C, PS_D = (ps[k] for k in ("ps_a", "ps_b", "ps_c", "ps_d"))
    x3T = nc.dram_tensor("x3T", [D, TC], F32, kind="ExternalInput").ap()
    xnfT = nc.dram_tensor("xnfT", [D, T], BF16, kind="ExternalInput").ap()
    basisT = nc.dram_tensor("basisT", [128, D], F32, kind="ExternalInput").ap()
    qcT = nc.dram_tensor("qcT", [128, 256], F32, kind="ExternalInput").ap()
    kcT = nc.dram_tensor("kcT", [128, 256], F32, kind="ExternalInput").ap()
    vcT = nc.dram_tensor("vcT", [128, 256], F32, kind="ExternalInput").ap()
    ocT = nc.dram_tensor("ocT", [128, 256], F32, kind="ExternalInput").ap()
    dlog = nc.dram_tensor("dlog", [1, 1], F32, kind="ExternalInput").ap()
    oscal = nc.dram_tensor("oscal", [1, 1], F32, kind="ExternalInput").ap()
    wqfT = nc.dram_tensor("wqfT", [D, D], F32, kind="ExternalInput").ap()
    wqaa = nc.dram_tensor("wqaa", [D, D], F32, kind="ExternalInput").ap()
    wqab = nc.dram_tensor("wqab", [D, D], F32, kind="ExternalInput").ap()
    wkaa = nc.dram_tensor("wkaa", [D, D], F32, kind="ExternalInput").ap()
    wkab = nc.dram_tensor("wkab", [D, D], F32, kind="ExternalInput").ap()
    wvaT = nc.dram_tensor("wvaT", [D, D], F32, kind="ExternalInput").ap()
    wrtT = nc.dram_tensor("wrtT", [D, 16], F32, kind="ExternalInput").ap()
    u3q = nc.dram_tensor("u3q", [128, TC], F32, kind="ExternalInput").ap()
    v3q = nc.dram_tensor("v3q", [128, TC], F32, kind="ExternalInput").ap()
    u3k = nc.dram_tensor("u3k", [128, T], F32, kind="ExternalInput").ap()
    v3k = nc.dram_tensor("v3k", [128, T], F32, kind="ExternalInput").ap()
    kfw = nc.dram_tensor("kfw", [16, 8, 128, 256], F32, kind="ExternalInput").ap()
    vfw = nc.dram_tensor("vfw", [16, 8, 256, 128], F32, kind="ExternalInput").ap()
    woutT = nc.dram_tensor("woutT", [2048, D], F32, kind="ExternalInput").ap()
    cmask = nc.dram_tensor("cmask", [5, 128, 512], F32, kind="ExternalInput").ap()
    grid3 = nc.dram_tensor("grid3", [128, 128], F32, kind="ExternalInput").ap()
    tri3 = nc.dram_tensor("tri3", [128, 128], F32, kind="ExternalInput").ap()
    ramp = nc.dram_tensor("ramp", [128, 1], F32, kind="ExternalInput").ap()
    rampr = nc.dram_tensor("rampr", [1, 128], F32, kind="ExternalInput").ap()
    vone3 = nc.dram_tensor("vone3", [T], F32, kind="ExternalInput").ap()
    kvmA = nc.dram_tensor("kvmA", [T], F32, kind="ExternalInput").ap()
    eye = nc.dram_tensor("eye", [128, 128], F32, kind="ExternalInput").ap()
    x4T = nc.dram_tensor("x4T", [D, TC], F32, kind="ExternalOutput").ap()
    K3D = nc.dram_tensor("K3D", [8, 128, T], BF16, kind="Internal").ap()
    V3D = nc.dram_tensor("V3D", [8, 64, 128, 128], BF16, kind="Internal").ap()
    katD = nc.dram_tensor("katD", [64, 128, 256], BF16, kind="Internal").ap()
    vatD = nc.dram_tensor("vatD", [64, 128, 256], BF16, kind="Internal").ap()
    katfD = nc.dram_tensor("katfD", [256, TC], BF16, kind="Internal").ap()
    qatD = nc.dram_tensor("qatD", [256, TC], BF16, kind="Internal").ap()
    qfD = nc.dram_tensor("qfD", [D, TC], BF16, kind="Internal").ap()
    q3rD = nc.dram_tensor("q3rD", [D, TC], BF16, kind="Internal").ap()
    retrD = nc.dram_tensor("retrD", [256, TC], BF16, kind="Internal").ap()
    ffwdD = nc.dram_tensor("ffwdD", [D, TC], BF16, kind="Internal").ap()
    at3D = nc.dram_tensor("at3D", [D, TC], BF16, kind="Internal").ap()
    kfbD = nc.dram_tensor("kfbD", [16, 8, 128, 256], BF16, kind="Internal").ap()
    vfbD = nc.dram_tensor("vfbD", [16, 8, 256, 128], BF16, kind="Internal").ap()
    g3TD = nc.dram_tensor("g3TD", [16, TC], F32, kind="Internal").ap()

    with est:
        onesc = ar.alloc([128, 128], F32)
        rowbuf = ar.alloc([128, 2048], F32)
        x2s = ar.alloc([128, 512], F32)
        xld = ar.alloc([128, 2, 512], F32)
        qsc = ar.alloc([128, 512], F32)
        scol = ar.alloc([128, 8], F32)
        srow = ar.alloc([128, 8], F32)
        dexp = ar.alloc([128, 128], F32)
        wint = ar.alloc([128, 128], F32)
        eyeb = ar.alloc([128, 128], F32)
        wscr = ar.alloc([128, 1024], F32)
        basb = ar.alloc([128, 1024], BF16)
        qcb = ar.alloc([128, 4, 256], BF16)
        qwb = ar.alloc([128, 8, 256], BF16)
        kwb = ar.alloc([128, 8, 256], BF16)
        vwb = ar.alloc([128, 8, 256], BF16)
        owtb = ar.alloc([128, 2, 1024], BF16)
        bscr = ar.alloc([128, 1024], F32)
        kvcol = ar.alloc([128, 64], F32)
        vocol = ar.alloc([128, 64], F32)
        vobf = ar.alloc([128, 64], BF16)

        tmz = ctx.sig("vector", lambda v: v.memset(onesc[:, :], 1.0))

        def cast_w(dst, src, cols=1024, ktiles=8):
            for ki in range(ktiles):
                t = ctx.dma("sync", wscr[:, 0:cols],
                            src[ki * 128:(ki + 1) * 128, :])
                ctx.wait("vector", "d_sync", t)
                tv = ctx.sig("vector", lambda v, ki=ki: v.tensor_copy(
                    out=dst[:, ki, 0:cols], in_=wscr[:, 0:cols]))
                ctx.wait("sync", "pv", tv)

        ctx.dma("sync", srow[0:1, 3:4], dlog[:, :])
        ctx.dma("sync", srow[0:1, 4:5], oscal[:, :])
        ctx.dma("sync", scol[:, 4:5], ramp[:, :])
        ctx.dma("sync", rowbuf[0:1, 0:128], rampr[:, :])
        ctx.dma("sync", xld[:, 0, 0:128], grid3[:, :])
        ctx.dma("sync", xld[:, 1, 0:128], tri3[:, :])
        ctx.dma("sync", eyeb[:, :], eye[:, :])
        tkv = ctx.dma("sync", kvcol[:, :],
                      kvmA.rearrange("(t p) -> p t", p=128),
                      allow_slow_non_contiguous=True)
        tvo_ = ctx.dma("sync", vocol[:, :],
                       vone3.rearrange("(t p) -> p t", p=128),
                       allow_slow_non_contiguous=True)
        ctx.wait("vector", "d_sync", tvo_)
        ctx.sig("vector", lambda v: v.tensor_copy(out=vobf[:, :],
                                                  in_=vocol[:, :]))
        ctx.wait("scalar", "d_sync", ctx.tok("d_sync"))
        ctx.sig("scalar", lambda s: s.activation(
            out=srow[0:1, 0:1], in_=srow[0:1, 3:4], func=AF.Sigmoid))
        ts1 = ctx.sig("scalar", lambda s: s.activation(
            out=srow[0:1, 1:2], in_=srow[0:1, 0:1], func=AF.Ln))
        ctx.wait("tensor", "ps", ts1)
        ctx.wait("tensor", "pv", tmz)
        ctx.sig("tensor", lambda t: t.matmul(
            PS_A[:, 0:1], onesc[0:1, :], srow[0:1, 1:2], start=True, stop=True))
        tb2 = ctx.sig("tensor", lambda t: t.matmul(
            PS_A[:, 1:2], onesc[0:1, :], srow[0:1, 4:5], start=True, stop=True))
        ctx.wait("vector", "pt", tb2)
        ctx.sig("vector", lambda v: v.tensor_copy(out=scol[:, 0:1],
                                                  in_=PS_A[:, 0:1]))
        tvv = ctx.sig("vector", lambda v: v.tensor_copy(out=scol[:, 3:4],
                                                        in_=PS_A[:, 1:2]))
        ctx.wait("scalar", "pv", tvv)
        ctx.sig("scalar", lambda s: s.activation(
            out=scol[:, 1:2], in_=scol[:, 4:5], func=AF.Exp,
            scale=scol[:, 0:1]))
        tvr = ctx.sig("vector", lambda v: v.tensor_scalar(
            out=srow[0:1, 2:3], in0=srow[0:1, 1:2], scalar1=128.0,
            scalar2=None, op0=OP.mult))
        ctx.wait("scalar", "pv", tvr)
        ctx.sig("scalar", lambda s: s.activation(
            out=srow[0:1, 5:6], in_=srow[0:1, 2:3], func=AF.Exp))
        tsw = ctx.sig("scalar", lambda s: s.activation(
            out=x2s[:, 0:128], in_=xld[:, 0, 0:128], func=AF.Exp,
            scale=scol[:, 0:1]))
        ctx.wait("vector", "ps", tsw)
        ctx.sig("vector", lambda v: v.tensor_tensor(
            out=wint[:, :], in0=x2s[:, 0:128], in1=xld[:, 1, 0:128],
            op=OP.mult))
        tsr = ctx.sig("scalar", lambda s: s.activation(
            out=rowbuf[0:1, 128:256], in_=rowbuf[0:1, 0:128], func=AF.Exp,
            scale=scol[0:1, 0:1]))
        ctx.wait("tensor", "ps", tsr)
        ctx.sig("tensor", lambda t: t.matmul(
            PS_B[:, 0:128], onesc[0:1, :], rowbuf[0:1, 128:256],
            start=True, stop=True))
        tb4 = ctx.sig("tensor", lambda t: t.matmul(
            PS_B[:, 128:129], onesc[0:1, :], srow[0:1, 5:6],
            start=True, stop=True))
        ctx.wait("vector", "pt", tb4)
        ctx.sig("vector", lambda v: v.tensor_copy(out=dexp[:, :],
                                                  in_=PS_B[:, 0:128]))
        ctx.sig("vector", lambda v: v.tensor_copy(out=scol[:, 2:3],
                                                  in_=PS_B[:, 128:129]))
        ctx.barrier()

        tbs = ctx.dma("sync", bscr[:, :], basisT[:, :])
        ctx.wait("vector", "d_sync", tbs)
        tv1 = ctx.sig("vector", lambda v: v.tensor_copy(out=basb[:, :],
                                                        in_=bscr[:, :]))
        ctx.wait("sync", "pv", tv1)
        for i, csrc in enumerate((qcT, kcT, vcT, ocT)):
            tc_ = ctx.dma("sync", bscr[:, 0:256], csrc[:, :])
            ctx.wait("vector", "d_sync", tc_)
            tv2 = ctx.sig("vector", lambda v, i=i: v.tensor_copy(
                out=qcb[:, i, :], in_=bscr[:, 0:256]))
            ctx.wait("sync", "pv", tv2)
        ctx.wait("tensor", "pv", ctx.tok("pv"))
        for i, wdst in enumerate((qwb, kwb, vwb)):
            for dt in range(8):
                tmm = ctx.sig("tensor", lambda t, i=i, dt=dt: t.matmul(
                    PS_A[:, 0:256], basb[:, dt * 128:(dt + 1) * 128],
                    qcb[:, i, :], start=True, stop=True))
                ctx.wait("vector", "pt", tmm)
                tv3 = ctx.sig("vector", lambda v, dt=dt, wdst=wdst:
                              v.tensor_copy(out=wdst[:, dt, :],
                                            in_=PS_A[:, 0:256]))
                ctx.wait("tensor", "pv", tv3)
        for ct_ in range(2):
            tmm = ctx.sig("tensor", lambda t, ct_=ct_: [
                t.matmul(PS_A[:, nh * 512:(nh + 1) * 512],
                         qcb[:, 3, ct_ * 128:(ct_ + 1) * 128],
                         basb[:, nh * 512:(nh + 1) * 512],
                         start=True, stop=True) for nh in range(2)][-1])
            ctx.wait("vector", "pt", tmm)
            tv4 = ctx.sig("vector", lambda v, ct_=ct_: v.tensor_scalar(
                out=owtb[:, ct_, :], in0=PS_A[:, 0:1024],
                scalar1=scol[:, 3:4], scalar2=None, op0=OP.mult))
            ctx.wait("tensor", "pv", tv4)
        for e in range(16):
            for h in range(8):
                t = ctx.dma("sync", wscr[:, 0:256], kfw[e, h, :, :])
                ctx.wait("vector", "d_sync", t)
                tv = ctx.sig("vector", lambda v: v.tensor_copy(
                    out=x2s[:, 0:512].bitcast(BF16)[:, 0:256],
                    in_=wscr[:, 0:256]))
                ctx.wait("gpsimd", "pv", tv)
                tg = ctx.dma("gpsimd", kfbD[e, h, :, :],
                             x2s[:, 0:512].bitcast(BF16)[:, 0:256])
                ctx.wait("vector", "d_gpsimd", tg)
                ctx.wait("sync", "d_gpsimd", tg)
                t1_ = ctx.dma("sync", wscr[:, 0:128], vfw[e, h, 0:128, :])
                t2_ = ctx.dma("sync", wscr[:, 128:256], vfw[e, h, 128:256, :])
                ctx.wait("vector", "d_sync", t2_)
                tv = ctx.sig("vector", lambda v: v.tensor_copy(
                    out=x2s[:, 0:512].bitcast(BF16)[:, 0:256],
                    in_=wscr[:, 0:256]))
                ctx.wait("gpsimd", "pv", tv)
                tg1 = ctx.dma("gpsimd", vfbD[e, h, 0:128, :],
                              x2s[:, 0:512].bitcast(BF16)[:, 0:128])
                tg2 = ctx.dma("gpsimd", vfbD[e, h, 128:256, :],
                              x2s[:, 0:512].bitcast(BF16)[:, 128:256])
                ctx.wait("vector", "d_gpsimd", tg2)
                ctx.wait("sync", "d_gpsimd", tg2)
        ctx.barrier()

        ar.mark()
        wka = ar.alloc([128, 8, 1024], BF16)
        wkb = ar.alloc([128, 8, 1024], BF16)
        wva = ar.alloc([128, 8, 1024], BF16)
        xnf = ar.alloc([128, 8, 512], BF16)
        ka3 = ar.alloc([128, 8, 512], BF16)
        kb3 = ar.alloc([128, 8, 512], BF16)
        kr3 = ar.alloc([128, 8, 512], BF16)
        utt = ar.alloc([128, 512], F32)
        vtt = ar.alloc([128, 512], F32)
        rsc = ar.alloc([128, 2, 512], F32)
        vtm3 = ar.alloc([128, 8, 128], BF16)
        katm = ar.alloc([128, 256], BF16)
        vatm = ar.alloc([128, 256], BF16)
        cast_w(wka, wkaa)
        cast_w(wkb, wkab)
        cast_w(wva, wvaT)
        ctx.barrier()
        for ch in range(16):
            c0 = ch * 512
            ctx.wait("sync", "pv", ctx.tok("pv"))
            ctx.wait("sync", "pt", ctx.tok("pt"))
            toks = []
            for ki in range(8):
                toks.append(ctx.dma("sync", xnf[:, ki, :],
                                    xnfT[ki * 128:(ki + 1) * 128, c0:c0 + 512]))
            tct = ctx.dma("sync", utt[:, :], u3k[:, c0:c0 + 512])
            tst = ctx.dma("sync", vtt[:, :], v3k[:, c0:c0 + 512])
            ctx.wait("tensor", "d_sync", toks[-1])
            for dt in range(8):
                for half_, (wsb, dst) in enumerate(((wka, ka3), (wkb, kb3))):
                    pj = PS_A if half_ == 0 else PS_B
                    tmm = ctx.sig("tensor", lambda t, dt=dt, wsb=wsb, pj=pj: [
                        t.matmul(pj[:, 0:512],
                                 wsb[:, ki, dt * 128:(dt + 1) * 128],
                                 xnf[:, ki, :], start=(ki == 0), stop=(ki == 7))
                        for ki in range(8)][-1])
                    ctx.wait("vector", "pt", tmm)
                    tv = ctx.sig("vector", lambda v, dt=dt, dst=dst, pj=pj:
                                 v.tensor_copy(out=dst[:, dt, :],
                                               in_=pj[:, 0:512]))
                    ctx.wait("tensor", "pv", tv)
            ctx.wait("vector", "d_sync", tst)
            for dt in range(8):
                ctx.sig("vector", lambda v, dt=dt: v.tensor_tensor(
                    out=rsc[:, 0, :], in0=ka3[:, dt, :], in1=utt[:, :],
                    op=OP.mult))
                ctx.sig("vector", lambda v, dt=dt: v.tensor_tensor(
                    out=rsc[:, 1, :], in0=kb3[:, dt, :], in1=vtt[:, :],
                    op=OP.mult))
                tvk = ctx.sig("vector", lambda v, dt=dt: v.tensor_tensor(
                    out=kr3[:, dt, :], in0=rsc[:, 0, :], in1=rsc[:, 1, :],
                    op=OP.add))
                ctx.wait("gpsimd", "pv", tvk)
                ctx.dma("gpsimd", K3D[dt, :, c0:c0 + 512], kr3[:, dt, :])
            for tt in range(4):
                gt = ch * 4 + tt
                pj = PS_C if tt % 2 == 0 else PS_D
                tmm = ctx.sig("tensor", lambda t, tt=tt, pj=pj: [
                    t.matmul(pj[:, nh * 512:(nh + 1) * 512],
                             xnf[:, ki, tt * 128:(tt + 1) * 128],
                             wva[:, ki, nh * 512:(nh + 1) * 512],
                             start=(ki == 0), stop=(ki == 7))
                    for nh in range(2) for ki in range(8)][-1])
                ctx.wait("vector", "pt", tmm)
                tvv = ctx.sig("vector", lambda v, tt=tt, gt=gt, pj=pj:
                              v.tensor_scalar(
                                  out=vtm3[:, :, :].rearrange(
                                      "p a b -> p (a b)"),
                                  in0=pj[:, 0:1024],
                                  scalar1=vocol[:, gt:gt + 1], scalar2=None,
                                  op0=OP.mult))
                ctx.wait("gpsimd", "pv", tvv)
                for h in range(8):
                    ctx.dma("gpsimd", V3D[h, gt, :, :], vtm3[:, h, :])
                ctx.wait("tensor", "pv", tvv)
                ctx.wait("vector", "d_gpsimd", ctx.tok("d_gpsimd"))
            for tt in range(4):
                gt = ch * 4 + tt
                pj = PS_C if tt % 2 == 0 else PS_D
                tmm = ctx.sig("tensor", lambda t, tt=tt, pj=pj: [
                    t.matmul(pj[:, w * 512:w * 512 + 256],
                             xnf[:, ki, tt * 128:(tt + 1) * 128],
                             (kwb if w == 0 else vwb)[:, ki, :],
                             start=(ki == 0), stop=(ki == 7))
                    for w in range(2) for ki in range(8)][-1])
                ctx.wait("vector", "pt", tmm)
                tvk = ctx.sig("vector", lambda v, gt=gt, pj=pj: v.tensor_scalar(
                    out=katm[:, :], in0=pj[:, 0:256],
                    scalar1=kvcol[:, gt:gt + 1], scalar2=None, op0=OP.mult))
                tvv2 = ctx.sig("vector", lambda v, gt=gt, pj=pj: v.tensor_scalar(
                    out=vatm[:, :], in0=pj[:, 512:768],
                    scalar1=kvcol[:, gt:gt + 1], scalar2=None, op0=OP.mult))
                ctx.wait("gpsimd", "pv", tvv2)
                ctx.dma("gpsimd", katD[gt, :, :], katm[:, :])
                ctx.dma("gpsimd", vatD[gt, :, :], vatm[:, :])
                ctx.wait("tensor", "pv", tvv2)
                ctx.wait("vector", "d_gpsimd", ctx.tok("d_gpsimd"))
        ar.release()
        ctx.barrier()

        ar.mark()
        xn_own = ar.alloc([128, 8, 2048], BF16)
        wq1 = ar.alloc([128, 8, 1024], BF16)
        wq2 = ar.alloc([128, 8, 1024], BF16)
        qa3 = ar.alloc([128, 8, 512], BF16)
        qb3 = ar.alloc([128, 8, 512], BF16)
        utt = ar.alloc([128, 512], F32)
        vtt = ar.alloc([128, 512], F32)
        rsc = ar.alloc([128, 2, 512], F32)
        g1 = ar.alloc([128, 64], F32)
        gmk = ar.alloc([128, 2], F32)
        gls = ar.alloc([128, 16], F32)
        for dt in range(8):
            for cc in range(4):
                ctx.dma("sync", xn_own[:, dt, cc * 512:(cc + 1) * 512],
                        xnfT[dt * 128:(dt + 1) * 128, cc * 512:(cc + 1) * 512])
        cast_w(wq1, wqfT)
        ctx.barrier()
        for tcol in range(4):
            for dt in range(8):
                pj = PS_A if dt % 2 == 0 else PS_B
                tmm = ctx.sig("tensor", lambda t, dt=dt, tcol=tcol, pj=pj: [
                    t.matmul(pj[:, 0:512],
                             wq1[:, ki, dt * 128:(dt + 1) * 128],
                             xn_own[:, ki, tcol * 512:(tcol + 1) * 512],
                             start=(ki == 0), stop=(ki == 7))
                    for ki in range(8)][-1])
                ctx.wait("vector", "pt", tmm)
                tv = ctx.sig("vector", lambda v, pj=pj: v.tensor_copy(
                    out=qsc[:, :].bitcast(BF16)[:, 0:512], in_=pj[:, 0:512]))
                ctx.wait("gpsimd", "pv", tv)
                tg = ctx.dma("gpsimd", qfD[dt * 128:(dt + 1) * 128,
                                           tcol * 512:(tcol + 1) * 512],
                             qsc[:, :].bitcast(BF16)[:, 0:512])
                ctx.wait("vector", "d_gpsimd", tg)
                ctx.wait("tensor", "pv", tv)
        for tcol in range(4):
            for w, dstD in ((0, qatD), (2, katfD)):
                pj = PS_A if w == 0 else PS_B
                wsel = qwb if w == 0 else kwb
                for ct_ in range(2):
                    tmm = ctx.sig("tensor", lambda t, tcol=tcol, ct_=ct_,
                                  wsel=wsel, pj=pj: [
                        t.matmul(pj[:, ct_ * 512:(ct_ + 1) * 512][:, 0:512],
                                 wsel[:, ki, ct_ * 128:(ct_ + 1) * 128],
                                 xn_own[:, ki, tcol * 512:(tcol + 1) * 512],
                                 start=(ki == 0), stop=(ki == 7))
                        for ki in range(8)][-1])
                    ctx.wait("vector", "pt", tmm)
                    tv = ctx.sig("vector", lambda v, ct_=ct_, pj=pj:
                                 v.tensor_copy(
                                     out=qsc[:, :].bitcast(BF16)[:, 0:512],
                                     in_=pj[:, ct_ * 512:(ct_ + 1) * 512]))
                    ctx.wait("gpsimd", "pv", tv)
                    tg = ctx.dma("gpsimd",
                                 dstD[ct_ * 128:(ct_ + 1) * 128,
                                      tcol * 512:(tcol + 1) * 512],
                                 qsc[:, :].bitcast(BF16)[:, 0:512])
                    ctx.wait("vector", "d_gpsimd", tg)
                    ctx.wait("tensor", "pv", tv)
        ctx.barrier()
        cast_w(wq1, wqaa)
        cast_w(wq2, wqab)
        ctx.barrier()
        for tcol in range(4):
            ctx.wait("sync", "pv", ctx.tok("pv"))
            tct = ctx.dma("sync", utt[:, :], u3q[:, tcol * 512:(tcol + 1) * 512])
            tst = ctx.dma("sync", vtt[:, :], v3q[:, tcol * 512:(tcol + 1) * 512])
            for dt in range(8):
                for half_, (wsb, dst) in enumerate(((wq1, qa3), (wq2, qb3))):
                    pj = PS_A if half_ == 0 else PS_B
                    tmm = ctx.sig("tensor", lambda t, dt=dt, tcol=tcol, wsb=wsb,
                                  pj=pj: [
                        t.matmul(pj[:, 0:512],
                                 wsb[:, ki, dt * 128:(dt + 1) * 128],
                                 xn_own[:, ki, tcol * 512:(tcol + 1) * 512],
                                 start=(ki == 0), stop=(ki == 7))
                        for ki in range(8)][-1])
                    ctx.wait("vector", "pt", tmm)
                    tv = ctx.sig("vector", lambda v, dt=dt, dst=dst, pj=pj:
                                 v.tensor_copy(out=dst[:, dt, :],
                                               in_=pj[:, 0:512]))
                    ctx.wait("tensor", "pv", tv)
            ctx.wait("vector", "d_sync", tst)
            for dt in range(8):
                ctx.sig("vector", lambda v, dt=dt: v.tensor_tensor(
                    out=rsc[:, 0, :], in0=qa3[:, dt, :], in1=utt[:, :],
                    op=OP.mult))
                ctx.sig("vector", lambda v, dt=dt: v.tensor_tensor(
                    out=rsc[:, 1, :], in0=qb3[:, dt, :], in1=vtt[:, :],
                    op=OP.mult))
                tvk = ctx.sig("vector", lambda v, dt=dt: v.tensor_tensor(
                    out=qsc[:, :], in0=rsc[:, 0, :], in1=rsc[:, 1, :],
                    op=OP.add))
                tvc = ctx.sig("vector", lambda v: v.tensor_copy(
                    out=rsc[:, 0, :].bitcast(BF16)[:, 0:512], in_=qsc[:, :]))
                ctx.wait("gpsimd", "pv", tvc)
                tg = ctx.dma("gpsimd", q3rD[dt * 128:(dt + 1) * 128,
                                            tcol * 512:(tcol + 1) * 512],
                             rsc[:, 0, :].bitcast(BF16)[:, 0:512])
                ctx.wait("vector", "d_gpsimd", tg)
        ctx.barrier()
        cast_w(wq1, wrtT, cols=16)
        ctx.barrier()
        for tt in range(16):
            tmm = ctx.sig("tensor", lambda t, tt=tt: [
                t.matmul(PS_A[:, 0:16], xn_own[:, ki, tt * 128:(tt + 1) * 128],
                         wq1[:, ki, 0:16], start=(ki == 0), stop=(ki == 7))
                for ki in range(8)][-1])
            ctx.wait("vector", "pt", tmm)
            tvl = ctx.sig("vector", lambda v: v.tensor_copy(
                out=gls[:, :], in_=PS_A[:, 0:16]))
            ctx.wait("scalar", "pv", tvl)
            tsg = ctx.sig("scalar", lambda s: s.activation(
                out=g1[:, 0:16], in_=gls[:, :], func=AF.Sigmoid))
            ctx.sig("vector", lambda v: v.reduce_max(
                out=gmk[:, 0:1], in_=gls[:, :], axis=AX))
            ctx.sig("vector", lambda v: v.tensor_scalar(
                out=x2s[:, 0:16], in0=gls[:, :], scalar1=gmk[:, 0:1],
                scalar2=None, op0=OP.is_ge))
            ctx.sig("vector", lambda v: v.scalar_tensor_tensor(
                out=g1[:, 32:48], in0=x2s[:, 0:16], scalar=-1e30,
                in1=gls[:, :], op0=OP.mult, op1=OP.add))
            ctx.sig("vector", lambda v: v.reduce_max(
                out=gmk[:, 1:2], in_=g1[:, 32:48], axis=AX))
            ctx.sig("vector", lambda v: v.tensor_scalar(
                out=x2s[:, 0:16], in0=gls[:, :], scalar1=gmk[:, 1:2],
                scalar2=None, op0=OP.is_ge))
            ctx.wait("vector", "ps", tsg)
            tvg = ctx.sig("vector", lambda v: v.tensor_tensor(
                out=g1[:, 16:32], in0=x2s[:, 0:16], in1=g1[:, 0:16],
                op=OP.mult))
            ctx.wait("tensor", "pv", tvg)
            ttr = ctx.sig("tensor", lambda t: t.transpose(
                PS_B[0:16, 0:128], g1[:, 16:32], eyeb[:, :]))
            ctx.wait("vector", "pt", ttr)
            tge = ctx.sig("vector", lambda v: v.tensor_copy(
                out=rsc[0:16, 0, 0:128], in_=PS_B[0:16, 0:128]))
            ctx.wait("gpsimd", "pv", tge)
            tgd = ctx.dma("gpsimd", g3TD[:, tt * 128:(tt + 1) * 128],
                          rsc[0:16, 0, 0:128])
            ctx.wait("tensor", "pv", tge)
            ctx.wait("vector", "d_gpsimd", tgd)
        ar.release()
        ctx.barrier()

        ar.mark()
        Mst = ar.alloc([128, 2, 2, 256], F32)
        kch = ar.alloc([128, 256], BF16)
        vch = ar.alloc([128, 256], BF16)
        vsch = ar.alloc([128, 256], BF16)
        qch = ar.alloc([128, 2, 128], BF16)
        kfch = ar.alloc([128, 2, 128], BF16)
        qtl = ar.alloc([128, 2, 128], F32)
        pt3 = ar.alloc([128, 128], BF16)
        ctx.do("vector", lambda v: v.memset(Mst[:, 0, :, :], 0.0))
        for cidx in range(64):
            c = 63 - cidx
            mi = cidx % 2
            mo = (cidx + 1) % 2
            ctx.wait("sync", "pt", ctx.tok("pt"))
            tkl = ctx.dma("sync", kch[:, :], katD[c, :, :])
            tvl = ctx.dma("sync", vch[:, :], vatD[c, :, :])
            own = c < 16
            if own:
                tql = ctx.dma("sync", qch[:, :, :],
                              qatD[:, c * 128:(c + 1) * 128]
                              .rearrange("(a p) t -> p a t", p=128))
                tkf = ctx.dma("sync", kfch[:, :, :],
                              katfD[:, c * 128:(c + 1) * 128]
                              .rearrange("(a p) t -> p a t", p=128))
                ctx.wait("vector", "d_sync", tkf)
                for a in range(2):
                    ctx.sig("vector", lambda v, a=a: v.tensor_tensor(
                        out=qtl[:, a, :], in0=qch[:, a, :], in1=dexp[:, :],
                        op=OP.mult))
                tq = ctx.tok("pv")
                ctx.wait("tensor", "pv", tq)
                for vct in range(2):
                    for kct in range(2):
                        ctx.sig("tensor", lambda t, vct=vct, kct=kct, mi=mi:
                                t.matmul(
                                    PS_C[:, vct * 128:(vct + 1) * 128],
                                    Mst[:, mi, kct, vct * 128:(vct + 1) * 128],
                                    qtl[:, kct, :],
                                    start=(vct == 0 and kct == 0), stop=False,
                                    skip_group_check=True))
                tsm = ctx.sig("tensor", lambda t, c=c: [
                    t.matmul(PS_D[:, 0:128], kfch[:, kct, :], qch[:, kct, :],
                             start=(kct == 0), stop=(kct == 1))
                    for kct in range(2)][-1])
                ctx.wait("vector", "pt", tsm)
                tpm = ctx.sig("vector", lambda v: v.tensor_tensor(
                    out=pt3[:, :], in0=PS_D[:, 0:128], in1=wint[:, :],
                    op=OP.mult))
                ctx.wait("tensor", "pv", tpm)
                ctx.wait("tensor", "d_sync", tvl)
                tpv = ctx.sig("tensor", lambda t: [
                    t.matmul(PS_C[:, vct * 128:(vct + 1) * 128],
                             vch[:, vct * 128:(vct + 1) * 128], pt3[:, :],
                             start=False, stop=(vct == 1),
                             skip_group_check=True)
                    for vct in range(2)][-1])
                ctx.wait("vector", "pt", tpv)
                tre = ctx.sig("vector", lambda v: v.tensor_copy(
                    out=qsc[:, 0:256].bitcast(BF16)[:, 0:256],
                    in_=PS_C[:, 0:256]))
                ctx.wait("gpsimd", "pv", tre)
                tg = ctx.dma("gpsimd",
                             retrD[:, c * 128:(c + 1) * 128]
                             .rearrange("(a p) t -> p a t", p=128),
                             qsc[:, 0:256].bitcast(BF16)[:, 0:256]
                             .rearrange("p (a t) -> p a t", t=128))
                ctx.wait("vector", "d_gpsimd", tg)
            ctx.wait("vector", "d_sync", tvl)
            tvs = ctx.sig("vector", lambda v: v.tensor_scalar(
                out=vsch[:, :], in0=vch[:, :], scalar1=scol[:, 1:2],
                scalar2=None, op0=OP.mult))
            ctx.wait("tensor", "pv", tvs)
            ctx.wait("tensor", "d_sync", tkl)
            for kct in range(2):
                tsm = ctx.sig("tensor", lambda t, kct=kct: t.matmul(
                    PS_A[:, kct * 256:(kct + 1) * 256],
                    kch[:, kct * 128:(kct + 1) * 128], vsch[:, :],
                    start=True, stop=True))
            tS = ctx.tok("pt")
            ctx.wait("vector", "pt", tS)
            for kct in range(2):
                tvm = ctx.sig("vector", lambda v, kct=kct, mi=mi, mo=mo:
                              v.scalar_tensor_tensor(
                                  out=Mst[:, mo, kct, :],
                                  in0=Mst[:, mi, kct, :],
                                  scalar=scol[:, 2:3],
                                  in1=PS_A[:, kct * 256:(kct + 1) * 256],
                                  op0=OP.mult, op1=OP.add))
            ctx.wait("tensor", "pv", ctx.tok("pv"))
            ctx.wait("sync", "pv", ctx.tok("pv"))
        ar.release()
        ctx.barrier()

        ar.mark()
        kfsb = ar.alloc([128, 2, 256], BF16)
        vfsb = ar.alloc([128, 2, 2, 128], BF16)
        qfsb = ar.alloc([128, 512], BF16)
        gesb = ar.alloc([128, 2, 512], BF16)
        g3bc = ar.alloc([128, 512], F32)
        gbr = ar.alloc([128, 2048], F32)
        sgb3 = ar.alloc([128, 2, 512], BF16)
        for tcol in range(4):
            tc0 = tcol * 512
            for h in range(8):
                ctx.wait("sync", "pt", ctx.tok("pt"))
                tqf = ctx.dma("sync", qfsb[:, :],
                              qfD[h * 128:(h + 1) * 128, tc0:tc0 + 512])
                for e in range(16):
                    ctx.wait("sync", "pt", ctx.tok("pt"))
                    tkf = ctx.dma("sync", kfsb[:, e % 2, :], kfbD[e, h, :, :])
                    tvf = ctx.dma("sync", vfsb[:, e % 2, :, :],
                                  vfbD[e, h, :, :]
                                  .rearrange("(a p) d -> p a d", p=128))
                    if e % 4 == 0:
                        tgb = ctx.dma("sync",
                                      gbr[0:1, :]
                                      .rearrange("p (e t) -> p e t", t=512),
                                      g3TD[None, e:e + 4, tc0:tc0 + 512])
                        ctx.wait("tensor", "d_sync", tgb)
                    tgm = ctx.sig("tensor", lambda t, e=e: t.matmul(
                        PS_B[:, 0:512], onesc[0:1, :],
                        gbr[0:1, (e % 4) * 512:(e % 4 + 1) * 512],
                        start=True, stop=True))
                    ctx.wait("vector", "pt", tgm)
                    tge = ctx.sig("vector", lambda v: v.tensor_copy(
                        out=g3bc[:, :], in_=PS_B[:, 0:512]))
                    ctx.wait("tensor", "d_sync", tkf)
                    ctx.wait("tensor", "d_sync", tqf)
                    ctx.wait("tensor", "pv", tge)
                    for st_ in range(2):
                        ctx.wait("tensor", "ps", ctx.tok("ps"))
                        tsm = ctx.sig("tensor", lambda t, e=e, st_=st_: t.matmul(
                            PS_A[:, st_ * 512:(st_ + 1) * 512],
                            kfsb[:, e % 2, st_ * 128:(st_ + 1) * 128],
                            qfsb[:, :], start=True, stop=True))
                        ctx.wait("scalar", "pt", tsm)
                        tsg = ctx.sig("scalar", lambda s, st_=st_: s.activation(
                            out=gesb[:, st_, :],
                            in_=PS_A[:, st_ * 512:(st_ + 1) * 512],
                            func=AF.Gelu))
                        ctx.wait("vector", "ps", tsg)
                        tsc = ctx.sig("vector", lambda v, st_=st_:
                                      v.tensor_tensor(
                                          out=sgb3[:, st_, :],
                                          in0=gesb[:, st_, :],
                                          in1=g3bc[:, :], op=OP.mult))
                        ctx.wait("tensor", "pv", tsc)
                    ctx.wait("tensor", "d_sync", tvf)
                    tout = ctx.sig("tensor", lambda t, e=e: [
                        t.matmul(PS_C[:, 0:512],
                                 vfsb[:, e % 2, st_, :], sgb3[:, st_, :],
                                 start=(e == 0 and st_ == 0),
                                 stop=(e == 15 and st_ == 1),
                                 skip_group_check=True)
                        for st_ in range(2)][-1])
                ctx.wait("vector", "pt", ctx.tok("pt"))
                tve = ctx.sig("vector", lambda v: v.tensor_copy(
                    out=qsc[:, :].bitcast(BF16)[:, 0:512], in_=PS_C[:, 0:512]))
                ctx.wait("gpsimd", "pv", tve)
                tg = ctx.dma("gpsimd", ffwdD[h * 128:(h + 1) * 128,
                                             tc0:tc0 + 512],
                             qsc[:, :].bitcast(BF16)[:, 0:512])
                ctx.wait("vector", "d_gpsimd", tg)
                ctx.wait("tensor", "pv", tve)
        ar.release()
        ctx.barrier()

        ar.mark()
        kst3 = ar.alloc([128, 2, 1024], BF16)
        vst3 = ar.alloc([128, 2, 8, 128], BF16)
        q3sb = ar.alloc([128, 512], BF16)
        pt6 = ar.alloc([128, 2, 512], BF16)
        es6 = ar.alloc([128, 512], F32)
        cm6 = ar.alloc([128, 5, 512], F32)
        bc6 = ar.alloc([128, 512], F32)
        for m in range(5):
            tcm = ctx.dma("sync", cm6[:, m, :], cmask[m, :, :])
        ctx.wait("vector", "d_sync", tcm)
        for h in range(8):
            for qc in range(4):
                ctx.wait("sync", "pt", ctx.tok("pt"))
                tql = ctx.dma("sync", q3sb[:, :],
                              q3rD[h * 128:(h + 1) * 128,
                                   qc * 512:(qc + 1) * 512])
                for kc in range(8):
                    bufi = kc % 2
                    ctx.wait("sync", "pt", ctx.tok("pt"))
                    tkl = ctx.dma("sync", kst3[:, bufi, :],
                                  K3D[h, :, kc * 1024:(kc + 1) * 1024])
                    tvl = ctx.dma("sync", vst3[:, bufi, :, :],
                                  V3D[h, kc * 8:kc * 8 + 8, :, :]
                                  .rearrange("n p c -> p n c"))
                    ctx.wait("tensor", "d_sync", tvl)
                    ctx.wait("tensor", "d_sync", tql)
                    for kt in range(8):
                        ktg = kc * 8 + kt
                        pj = PS_A if ktg % 2 == 0 else PS_B
                        tmm = ctx.sig("tensor", lambda t, kt=kt, bufi=bufi,
                                      pj=pj: t.matmul(
                                          pj[:, 0:512],
                                          kst3[:, bufi, kt * 128:(kt + 1) * 128],
                                          q3sb[:, :], start=True, stop=True))
                        cls = None
                        if ktg < 16:
                            m = ktg - 4 * qc
                            if 0 <= m < 4:
                                cls = m
                            elif m >= 4:
                                cls = 4
                        if cls is not None:
                            ctx.wait("vector", "pt", tmm)
                            tvm = ctx.sig("vector", lambda v, cls=cls, pj=pj:
                                          v.tensor_tensor(
                                              out=es6[:, :], in0=pj[:, 0:512],
                                              in1=cm6[:, cls, :], op=OP.add))
                            ctx.wait("scalar", "pv", tvm)
                            tex = ctx.sig("scalar", lambda s, kt=kt:
                                          s.activation(
                                              out=pt6[:, kt % 2, :],
                                              in_=es6[:, :], func=AF.Exp,
                                              scale=0.08838834764831845))
                            ctx.wait("vector", "ps", tex)
                        else:
                            ctx.wait("scalar", "pt", tmm)
                            tex = ctx.sig("scalar", lambda s, kt=kt, pj=pj:
                                          s.activation(
                                              out=pt6[:, kt % 2, :],
                                              in_=pj[:, 0:512], func=AF.Exp,
                                              scale=0.08838834764831845))
                        ctx.wait("tensor", "ps", tex)
                        tpv = ctx.sig("tensor", lambda t, kt=kt, bufi=bufi,
                                      ktg=ktg: t.matmul(
                                          PS_C[:, 0:512],
                                          vst3[:, bufi, kt, :],
                                          pt6[:, kt % 2, :],
                                          start=(ktg == 0), stop=(ktg == 63)))
                        tdn = ctx.sig("tensor", lambda t, kt=kt, bufi=bufi,
                                      ktg=ktg, kc=kc: t.matmul(
                                          PS_D[0:1, 0:512],
                                          vobf[:, kc * 8 + kt:kc * 8 + kt + 1],
                                          pt6[:, kt % 2, :],
                                          start=(ktg == 0), stop=(ktg == 63)))
                        ctx.wait("scalar", "pt", tdn)
                ctx.wait("vector", "pt", ctx.tok("pt"))
                tv1 = ctx.sig("vector", lambda v: v.reciprocal(
                    out=rowbuf[0:1, 0:512], in_=PS_D[0:1, 0:512]))
                ctx.wait("tensor", "pv", tv1)
                tb = ctx.sig("tensor", lambda t: t.matmul(
                    PS_B[:, 0:512], onesc[0:1, :], rowbuf[0:1, 0:512],
                    start=True, stop=True))
                ctx.wait("vector", "pt", tb)
                tv2 = ctx.sig("vector", lambda v: v.tensor_copy(
                    out=bc6[:, :], in_=PS_B[:, 0:512]))
                tv3 = ctx.sig("vector", lambda v: v.tensor_tensor(
                    out=qsc[:, :].bitcast(BF16)[:, 0:512],
                    in0=PS_C[:, 0:512], in1=bc6[:, :], op=OP.mult))
                ctx.wait("gpsimd", "pv", tv3)
                tg = ctx.dma("gpsimd", at3D[h * 128:(h + 1) * 128,
                                            qc * 512:(qc + 1) * 512],
                             qsc[:, :].bitcast(BF16)[:, 0:512])
                ctx.wait("vector", "d_gpsimd", tg)
                ctx.wait("tensor", "pv", tv3)
        ar.release()
        ctx.barrier()

        ar.mark()
        wob = ar.alloc([128, 16, 1024], BF16)
        a3s = ar.alloc([128, 8, 512], BF16)
        ff3s = ar.alloc([128, 8, 512], BF16)
        rt3s = ar.alloc([128, 2, 512], BF16)
        xres = ar.alloc([128, 512], F32)
        x4ev = ar.alloc([128, 512], F32)
        cast_w(wob, woutT, ktiles=16)
        ctx.barrier()
        for tcol in range(4):
            tc0 = tcol * 512
            ctx.wait("sync", "pt", ctx.tok("pt"))
            for cc in range(8):
                ctx.dma("sync", a3s[:, cc, :],
                        at3D[cc * 128:(cc + 1) * 128, tc0:tc0 + 512])
                ctx.dma("sync", ff3s[:, cc, :],
                        ffwdD[cc * 128:(cc + 1) * 128, tc0:tc0 + 512])
            for cc in range(2):
                ctx.dma("sync", rt3s[:, cc, :],
                        retrD[cc * 128:(cc + 1) * 128, tc0:tc0 + 512])
            tld = ctx.tok("d_sync")
            ctx.wait("tensor", "d_sync", tld)
            for do in range(8):
                pj = PS_A if do % 2 == 0 else PS_B

                def mmo(t, do=do, tcol=tcol, pj=pj):
                    ins = None
                    for cc in range(8):
                        ins = t.matmul(pj[:, 0:512],
                                       wob[:, cc, do * 128:(do + 1) * 128],
                                       a3s[:, cc, :],
                                       start=(cc == 0), stop=False)
                    for cc in range(8):
                        ins = t.matmul(pj[:, 0:512],
                                       wob[:, 8 + cc, do * 128:(do + 1) * 128],
                                       ff3s[:, cc, :], start=False, stop=False)
                    for cc in range(2):
                        ins = t.matmul(pj[:, 0:512],
                                       owtb[:, cc, do * 128:(do + 1) * 128],
                                       rt3s[:, cc, :], start=False,
                                       stop=(cc == 1))
                    return ins
                tmm = ctx.sig("tensor", mmo)
                tx = ctx.dma("sync", xres[:, :],
                             x3T[do * 128:(do + 1) * 128, tc0:tc0 + 512])
                ctx.wait("vector", "pt", tmm)
                ctx.wait("vector", "d_sync", tx)
                tv = ctx.sig("vector", lambda v, pj=pj: v.tensor_tensor(
                    out=x4ev[:, :], in0=pj[:, 0:512], in1=xres[:, :],
                    op=OP.add))
                ctx.wait("gpsimd", "pv", tv)
                ctx.wait("sync", "pv", tv)
                tst = ctx.dma("gpsimd", x4T[do * 128:(do + 1) * 128,
                                            tc0:tc0 + 512], x4ev[:, :])
                ctx.wait("vector", "d_gpsimd", tst)
                ctx.wait("tensor", "pv", tv)
        ar.release()
        ctx.barrier()
        ctx.wait("gpsimd", "d_gpsimd", ctx.tok("d_gpsimd"))
        with nc.Block() as block:
            ctx.replay(block)
    return nc


def l3_in_maps(inp, x3T_list, xn3_list):
    u3, v3 = l3_tables()
    cm = l3_cmask()
    g3 = l3_grid()
    w_in = np.asarray(inp["L3_ffn_w_in"])
    E = 1024
    sig1 = np.concatenate([h * 128 + np.concatenate(
        [np.arange(64), np.arange(64)]) for h in range(8)])
    sig2 = np.concatenate([h * 128 + np.concatenate(
        [64 + np.arange(64), 64 + np.arange(64)]) for h in range(8)])
    wqf = w_in[0:E]
    wqa = w_in[E:2 * E]
    wka = w_in[2 * E:3 * E]
    wva = w_in[3 * E:4 * E]
    wrt = w_in[4 * E:4 * E + 16]
    base = {
        "basisT": np.ascontiguousarray(np.asarray(inp["L3_attn_basis"]).T),
        "qcT": np.ascontiguousarray(np.asarray(inp["L3_attn_q_coeffs"]).T),
        "kcT": np.ascontiguousarray(np.asarray(inp["L3_attn_k_coeffs"]).T),
        "vcT": np.ascontiguousarray(np.asarray(inp["L3_attn_v_coeffs"]).T),
        "ocT": np.ascontiguousarray(np.asarray(inp["L3_attn_o_coeffs"]).T),
        "dlog": np.asarray(inp["L3_attn_decay_logit"]).reshape(1, 1),
        "oscal": np.asarray(inp["L3_attn_out_scale"]).reshape(1, 1),
        "wqfT": np.ascontiguousarray(wqf.T),
        "wqaa": np.ascontiguousarray(wqa[sig1].T),
        "wqab": np.ascontiguousarray(wqa[sig2].T),
        "wkaa": np.ascontiguousarray(wka[sig1].T),
        "wkab": np.ascontiguousarray(wka[sig2].T),
        "wvaT": np.ascontiguousarray(wva.T),
        "wrtT": np.ascontiguousarray(wrt.T),
        "kfw": np.ascontiguousarray(
            np.asarray(inp["L3_ffn_k_ffwd"]).transpose(0, 1, 3, 2)),
        "vfw": np.ascontiguousarray(np.asarray(inp["L3_ffn_v_ffwd"])),
        "woutT": np.ascontiguousarray(np.asarray(inp["L3_ffn_w_out"]).T),
        "cmask": cm,
        "grid3": g3,
        "tri3": (np.arange(128)[:, None] > np.arange(128)[None, :])
        .astype(np.float32),
        "ramp": np.arange(128, dtype=np.float32).reshape(128, 1),
        "rampr": (127.0 - np.arange(128, dtype=np.float32)).reshape(1, 128),
        "eye": np.eye(128, dtype=np.float32),
    }
    maps = []
    for c in range(NCORE):
        b, q = c // 4, c % 4
        t0 = q * TC
        rot = np.roll(np.arange(T), -t0)
        pos = rot
        von = (pos <= t0 + TC - 1).astype(np.float32)
        kvm = (pos >= t0).astype(np.float32)
        xnb = np.concatenate([np.asarray(xn3_list[b * 4 + j]) for j in range(4)],
                             axis=1)
        m = dict(base)
        m["x3T"] = np.asarray(x3T_list[c])
        m["xnfT"] = np.ascontiguousarray(xnb[:, rot])
        m["u3q"] = np.ascontiguousarray(u3[:, t0:t0 + TC])
        m["v3q"] = np.ascontiguousarray(v3[:, t0:t0 + TC])
        m["u3k"] = np.ascontiguousarray(u3[:, rot])
        m["v3k"] = np.ascontiguousarray(v3[:, rot])
        m["vone3"] = von
        m["kvmA"] = kvm
        maps.append(m)
    return maps




def build_L2(dbg=False):
    psums = [("ps_a", [128, 1024], F32),
             ("ps_b", [128, 1024], F32),
             ("ps_c", [128, 1024], F32),
             ("ps_d", [128, 1024], F32)]
    nc, ctx, ar, ps, est = _prog(190000, psums)
    PS_A, PS_B, PS_C, PS_D = (ps[k] for k in ("ps_a", "ps_b", "ps_c", "ps_d"))
    HH = 2816
    xnfT2 = nc.dram_tensor("xnfT2", [D, T], BF16, kind="ExternalInput").ap()
    xnF = nc.dram_tensor("xnF", [D, TC], F32, kind="ExternalInput").ap()
    x2T = nc.dram_tensor("x2T", [D, TC], F32, kind="ExternalInput").ap()
    winX = nc.dram_tensor("winX", [D, 512], F32, kind="ExternalInput").ap()
    winZ = nc.dram_tensor("winZ", [D, 512], F32, kind="ExternalInput").ap()
    wcv = nc.dram_tensor("wcv", [128, 16], F32, kind="ExternalInput").ap()
    bcv = nc.dram_tensor("bcv", [128, 4], F32, kind="ExternalInput").ap()
    wxp = nc.dram_tensor("wxp", [512, 96], F32, kind="ExternalInput").ap()
    wdtw = nc.dram_tensor("wdtw", [64, 512], F32, kind="ExternalInput").ap()
    bdtc = nc.dram_tensor("bdtc", [128, 4], F32, kind="ExternalInput").ap()
    alog = nc.dram_tensor("alog", [128, 64], F32, kind="ExternalInput").ap()
    dcolI = nc.dram_tensor("dcolI", [128, 4], F32, kind="ExternalInput").ap()
    woutI = nc.dram_tensor("woutI", [512, D], F32, kind="ExternalInput").ap()
    wg2T = nc.dram_tensor("wg2T", [D, HH], BF16, kind="ExternalInput").ap()
    wu2T = nc.dram_tensor("wu2T", [D, HH], BF16, kind="ExternalInput").ap()
    wd2T = nc.dram_tensor("wd2T", [HH, D], BF16, kind="ExternalInput").ap()
    eye = nc.dram_tensor("eye", [128, 128], F32, kind="ExternalInput").ap()
    x3T = nc.dram_tensor("x3T", [D, TC], F32, kind="ExternalOutput").ap()
    xn3T = nc.dram_tensor("xn3T", [D, TC], BF16, kind="ExternalOutput").ap()
    ik = "ExternalOutput" if dbg else "Internal"
    xinD = nc.dram_tensor("xinD", [512, T], F32, kind=ik).ap()
    szD = nc.dram_tensor("szD", [512, T], F32, kind=ik).ap()
    xsD = nc.dram_tensor("xsD", [512, T], F32, kind=ik).ap()
    dtD = nc.dram_tensor("dtD", [512, T], F32, kind=ik).ap()
    yD = nc.dram_tensor("yD", [512, T], F32, kind=ik).ap()
    dbcFi = nc.dram_tensor("dbcFi", [96, T], F32, kind="Internal").ap()
    dbcFo = nc.dram_tensor("dbcFo", [96, T], F32, kind="Internal").ap()
    dbcMi = nc.dram_tensor("dbcMi", [T, 96], F32, kind="Internal").ap()
    dbcMo = nc.dram_tensor("dbcMo", [T, 96], F32, kind="Internal").ap()
    RSi = nc.dram_tensor("RSi", [4, D, TC], F32, kind="Internal").ap()
    attnD = nc.dram_tensor("attnD", [D, TC], F32, kind="Internal").ap()
    if dbg:
        dbgA = nc.dram_tensor("dbgA", [2, 128, 4096], F32,
                              kind="ExternalOutput").ap()
        dbgU = nc.dram_tensor("dbgU", [2, 128, 4096], F32,
                              kind="ExternalOutput").ap()
        dbgS = nc.dram_tensor("dbgS", [2, 128, 4096], F32,
                              kind="ExternalOutput").ap()
        dbcFoX = nc.dram_tensor("dbcFoX", [96, T], F32,
                                kind="ExternalOutput").ap()
        xqDX = nc.dram_tensor("xqDX", [D, TC], BF16,
                              kind="ExternalOutput").ap()
        dbcMoX = nc.dram_tensor("dbcMoX", [T, 96], F32,
                                kind="ExternalOutput").ap()
        attnDX = nc.dram_tensor("attnDX", [D, TC], F32,
                                kind="ExternalOutput").ap()
    hqD = nc.dram_tensor("hqD", [HH, TC], BF16, kind=ik).ap()

    with est:
        onesc = ar.alloc([128, 128], F32)
        onecol = ar.alloc([128, 1], F32)
        rowbuf = ar.alloc([128, 2048], F32)
        xld = ar.alloc([128, 2, 512], F32)
        x2s = ar.alloc([128, 512], F32)
        qsc = ar.alloc([128, 512], F32)
        wscr = ar.alloc([128, 1024], F32)
        wxbx = ar.alloc([128, 8, 512], BF16)
        wxbz = ar.alloc([128, 8, 512], BF16)
        wxpb = ar.alloc([128, 4, 96], BF16)
        wdtb = ar.alloc([128, 512], BF16)
        woutb = ar.alloc([128, 4, 1024], BF16)
        wcvb = ar.alloc([128, 16], F32)
        bcvb = ar.alloc([128, 4], F32)
        bdtb = ar.alloc([128, 4], F32)
        dcolb = ar.alloc([128, 4], F32)
        negA = ar.alloc([128, 4, 16], F32)
        eyeb = ar.alloc([128, 128], F32)
        eyebb = ar.alloc([128, 128], BF16)
        szero = ar.alloc([128, 4, 16], F32)

        tmz = ctx.sig("vector", lambda v: v.memset(onesc[:, :], 1.0))
        ctx.sig("vector", lambda v: v.memset(onecol[:, :], 1.0))
        tmz2 = ctx.sig("vector", lambda v: v.memset(szero[:, :, :], 0.0))

        def cast_w(dst, src, cols=1024, ktiles=8):
            for ki in range(ktiles):
                t = ctx.dma("sync", wscr[:, 0:cols],
                            src[ki * 128:(ki + 1) * 128, :])
                ctx.wait("vector", "d_sync", t)
                tv = ctx.sig("vector", lambda v, ki=ki: v.tensor_copy(
                    out=dst[:, ki, 0:cols], in_=wscr[:, 0:cols]))
                ctx.wait("sync", "pv", tv)

        cast_w(wxbx, winX, cols=512)
        cast_w(wxbz, winZ, cols=512)
        cast_w(wxpb, wxp, cols=96, ktiles=4)
        cast_w(woutb, woutI, cols=1024, ktiles=4)
        tw = ctx.dma("sync", wscr[0:64, 0:512], wdtw[:, :])
        ctx.wait("vector", "d_sync", tw)
        tv = ctx.sig("vector", lambda v: v.tensor_copy(
            out=wdtb[0:64, :], in_=wscr[0:64, 0:512]))
        ctx.wait("sync", "pv", tv)
        ctx.dma("sync", wcvb[:, :], wcv[:, :])
        ctx.dma("sync", bcvb[:, :], bcv[:, :])
        ctx.dma("sync", bdtb[:, :], bdtc[:, :])
        ctx.dma("sync", dcolb[:, :], dcolI[:, :])
        ctx.dma("sync", eyeb[:, :], eye[:, :])
        ta_ = ctx.dma("sync", wscr[:, 0:64], alog[:, :])
        ctx.wait("scalar", "d_sync", ta_)
        ctx.sig("scalar", lambda s: s.activation(
            out=negA[:, :, :].rearrange("p a b -> p (a b)"),
            in_=wscr[:, 0:64], func=AF.Exp))
        ctx.wait("vector", "d_sync", ctx.tok("d_sync"))
        ctx.sig("vector", lambda v: v.tensor_copy(out=eyebb[:, :],
                                                  in_=eyeb[:, :]))
        ctx.barrier()

        ar.mark()
        xnf = ar.alloc([128, 8, 512], BF16)
        ev1 = ar.alloc([128, 512], F32)
        for tcol in range(16):
            c0 = tcol * 512
            ctx.wait("sync", "pt", ctx.tok("pt"))
            for ki in range(8):
                ctx.dma("sync", xnf[:, ki, :],
                        xnfT2[ki * 128:(ki + 1) * 128, c0:c0 + 512])
            tld = ctx.tok("d_sync")
            ctx.wait("tensor", "d_sync", tld)
            for half_, wsb in ((0, wxbx), (1, wxbz)):
                for g in range(4):
                    pj = (PS_A, PS_B, PS_C, PS_D)[g]
                    tmm = ctx.sig("tensor", lambda t, g=g, wsb=wsb, pj=pj: [
                        t.matmul(pj[:, 0:512],
                                 wsb[:, ki, g * 128:(g + 1) * 128],
                                 xnf[:, ki, :], start=(ki == 0), stop=(ki == 7))
                        for ki in range(8)][-1])
                    if half_ == 0:
                        ctx.wait("vector", "pt", tmm)
                        te = ctx.sig("vector", lambda v, pj=pj: v.tensor_copy(
                            out=ev1[:, :], in_=pj[:, 0:512]))
                        dstD = xinD
                    else:
                        ctx.wait("scalar", "pt", tmm)
                        te = ctx.sig("scalar", lambda s, pj=pj: s.activation(
                            out=ev1[:, :], in_=pj[:, 0:512], func=AF.Silu))
                        dstD = szD
                    ctx.wait("gpsimd", "pv" if half_ == 0 else "ps", te)
                    tg = ctx.dma("gpsimd",
                                 dstD[g * 128:(g + 1) * 128, c0:c0 + 512],
                                 ev1[:, :])
                    ctx.wait("vector", "d_gpsimd", tg)
                    ctx.wait("scalar", "d_gpsimd", tg)
                    ctx.wait("tensor", "pv" if half_ == 0 else "ps", te)
        ar.release()
        ctx.barrier()

        ar.mark()
        xw = ar.alloc([128, 4, 520], F32)
        ca = ar.alloc([128, 512], F32)
        cb = ar.alloc([128, 512], F32)
        xsb = ar.alloc([128, 4, 512], F32)
        xsbb = ar.alloc([128, 4, 512], BF16)
        ev96 = ar.alloc([128, 96], F32)
        for tcol in range(16):
            c0 = tcol * 512
            ctx.wait("sync", "pv", ctx.tok("pv"))
            ctx.wait("sync", "ps", ctx.tok("ps"))
            if tcol == 0:
                tz_ = ctx.sig("vector", lambda v: v.memset(xw[:, :, 0:3], 0.0))
                ctx.wait("sync", "pv", tz_)
            for g in range(4):
                if c0 > 0:
                    ctx.dma("sync", xw[:, g, 0:3],
                            xinD[g * 128:(g + 1) * 128, c0 - 3:c0])
                ctx.dma("sync", xw[:, g, 3:515],
                        xinD[g * 128:(g + 1) * 128, c0:c0 + 512])
            tld = ctx.tok("d_sync")
            ctx.wait("vector", "d_sync", tld)
            for g in range(4):
                ctx.sig("vector", lambda v, g=g: v.tensor_scalar(
                    out=ca[:, :], in0=xw[:, g, 0:512],
                    scalar1=wcvb[:, g * 4:g * 4 + 1], scalar2=None,
                    op0=OP.mult))
                ctx.sig("vector", lambda v, g=g: v.scalar_tensor_tensor(
                    out=cb[:, :], in0=xw[:, g, 1:513],
                    scalar=wcvb[:, g * 4 + 1:g * 4 + 2], in1=ca[:, :],
                    op0=OP.mult, op1=OP.add))
                ctx.sig("vector", lambda v, g=g: v.scalar_tensor_tensor(
                    out=ca[:, :], in0=xw[:, g, 2:514],
                    scalar=wcvb[:, g * 4 + 2:g * 4 + 3], in1=cb[:, :],
                    op0=OP.mult, op1=OP.add))
                tcv = ctx.sig("vector", lambda v, g=g: v.scalar_tensor_tensor(
                    out=cb[:, :], in0=xw[:, g, 3:515],
                    scalar=wcvb[:, g * 4 + 3:g * 4 + 4], in1=ca[:, :],
                    op0=OP.mult, op1=OP.add))
                ctx.wait("scalar", "pv", tcv)
                tsl = ctx.sig("scalar", lambda s, g=g: s.activation(
                    out=xsb[:, g, :], in_=cb[:, :], func=AF.Silu,
                    bias=bcvb[:, g:g + 1]))
                ctx.wait("vector", "ps", tsl)
            tvc = ctx.sig("vector", lambda v: v.tensor_copy(
                out=xsbb[:, :, :], in_=xsb[:, :, :]))
            ctx.wait("gpsimd", "pv", tvc)
            for g in range(4):
                ctx.dma("gpsimd", xsD[g * 128:(g + 1) * 128, c0:c0 + 512],
                        xsb[:, g, :])
            ctx.wait("tensor", "pv", tvc)
            tmm = ctx.sig("tensor", lambda t: [
                t.matmul(PS_A[0:96, 0:512], wxpb[:, g, :], xsbb[:, g, :],
                         start=(g == 0), stop=(g == 3)) for g in range(4)][-1])
            ctx.wait("vector", "pt", tmm)
            tev = ctx.sig("vector", lambda v: v.tensor_copy(
                out=qsc[0:96, :], in_=PS_A[0:96, 0:512]))
            ctx.wait("gpsimd", "pv", tev)
            tg1 = ctx.dma("gpsimd", dbcFi[:, c0:c0 + 512], qsc[0:96, :])
            ctx.wait("vector", "d_gpsimd", tg1)
            ctx.wait("tensor", "pv", tev)
            for tt in range(4):
                tmm = ctx.sig("tensor", lambda t, tt=tt: [
                    t.matmul(PS_B[:, 0:96],
                             xsbb[:, g, tt * 128:(tt + 1) * 128],
                             wxpb[:, g, :], start=(g == 0), stop=(g == 3))
                    for g in range(4)][-1])
                ctx.wait("vector", "pt", tmm)
                tev = ctx.sig("vector", lambda v: v.tensor_copy(
                    out=ev96[:, :], in_=PS_B[:, 0:96]))
                ctx.wait("gpsimd", "pv", tev)
                tg2 = ctx.dma("gpsimd",
                              dbcMi[c0 + tt * 128:c0 + (tt + 1) * 128, :],
                              ev96[:, :])
                ctx.wait("vector", "d_gpsimd", tg2)
                ctx.wait("tensor", "pv", tev)
        ar.release()
        ctx.barrier()

        ctx.wait("gpsimd", "d_gpsimd", ctx.tok("d_gpsimd"))
        ctx.wait("gpsimd", "d_sync", ctx.tok("d_sync"))
        tc1 = ctx.sig("gpsimd", lambda g: g.collective_compute(
            "AllReduce", OP.add, ins=[dbcFi.opt()], outs=[dbcFo.opt()],
            replica_groups=[[0, 1, 2, 3], [4, 5, 6, 7]]))
        tc2 = ctx.sig("gpsimd", lambda g: g.collective_compute(
            "AllReduce", OP.add, ins=[dbcMi.opt()], outs=[dbcMo.opt()],
            replica_groups=[[0, 1, 2, 3], [4, 5, 6, 7]]))
        ctx.wait("gpsimd", "pp", tc2)
        if dbg:
            ctx.dma("gpsimd", dbcFoX[:, :], dbcFo[:, :])
            ctx.dma("gpsimd", dbcMoX[:, :], dbcMo[:, :])
            ctx.wait("gpsimd", "d_gpsimd", ctx.tok("d_gpsimd"))
        ctx.barrier()

        ar.mark()
        dbf = ar.alloc([128, 512], F32)
        dbfb = ar.alloc([128, 512], BF16)
        t1b = ar.alloc([128, 512], F32)
        t2b = ar.alloc([128, 512], F32)
        t3b = ar.alloc([128, 512], F32)
        for tcol in range(16):
            c0 = tcol * 512
            ctx.wait("sync", "pv", ctx.tok("pv"))
            tld = ctx.dma("sync", dbf[0:64, :], dbcFo[0:64, c0:c0 + 512])
            ctx.wait("vector", "d_sync", tld)
            tvb = ctx.sig("vector", lambda v: v.tensor_copy(
                out=dbfb[0:64, :], in_=dbf[0:64, :]))
            ctx.wait("tensor", "pv", tvb)
            for g in range(4):
                pj = PS_A if g % 2 == 0 else PS_B
                tmm = ctx.sig("tensor", lambda t, g=g, pj=pj: t.matmul(
                    pj[:, 0:512], wdtb[0:64, g * 128:(g + 1) * 128],
                    dbfb[0:64, :], start=True, stop=True))
                ctx.wait("vector", "pt", tmm)
                tv1 = ctx.sig("vector", lambda v, g=g, pj=pj: v.tensor_scalar(
                    out=t1b[:, :], in0=pj[:, 0:512],
                    scalar1=bdtb[:, g:g + 1], scalar2=80.0,
                    op0=OP.add, op1=OP.min))
                ctx.wait("scalar", "pv", tv1)
                ts1 = ctx.sig("scalar", lambda s: s.activation(
                    out=t2b[:, :], in_=t1b[:, :], func=AF.Exp))
                ts2 = ctx.sig("scalar", lambda s: s.activation(
                    out=t3b[:, :], in_=t2b[:, :], func=AF.Ln,
                    bias=onecol[:, 0:1]))
                ctx.wait("gpsimd", "ps", ts2)
                tg = ctx.dma("gpsimd", dtD[g * 128:(g + 1) * 128, c0:c0 + 512],
                             t3b[:, :])
                ctx.wait("scalar", "d_gpsimd", tg)
                ctx.wait("tensor", "pv", tv1)
                ctx.wait("vector", "ps", ts2)
        ar.release()
        ctx.barrier()

        ar.mark()
        dtb = ar.alloc([128, 4, 64], F32)
        xsb5 = ar.alloc([128, 4, 64], F32)
        xdtb = ar.alloc([128, 64, 4], F32)
        zb = ar.alloc([128, 64, 4, 16], F32)
        zb2 = ar.alloc([128, 64, 4, 16], F32)
        dAb = ar.alloc([128, 64, 4, 16], F32)
        ub = ar.alloc([128, 64, 4, 16], F32)
        sh = ar.alloc([128, 2, 64, 4, 16], F32)
        yt4 = ar.alloc([128, 64, 4, 16], F32)
        ybg = ar.alloc([128, 4, 64], F32)
        sa = ar.alloc([128, 4, 16], F32)
        sb = ar.alloc([128, 4, 16], F32)
        bbuf = ar.alloc([128, 1024], F32)
        cbuf = ar.alloc([128, 1024], F32)
        for ci in range(128):
            c0 = ci * 64
            pb = ci % 2
            ctx.wait("sync", "pv", ctx.tok("pv"))
            ctx.wait("sync", "ps", ctx.tok("ps"))
            ctx.wait("sync", "pt", ctx.tok("pt"))
            for g in range(4):
                ctx.dma("sync", dtb[:, g, :],
                        dtD[g * 128:(g + 1) * 128, c0:c0 + 64])
                ctx.dma("sync", xsb5[:, g, :],
                        xsD[g * 128:(g + 1) * 128, c0:c0 + 64])
            ctx.dma("sync", rowbuf[0:1, 0:1024]
                    .rearrange("o (t n) -> o t n", n=16),
                    dbcMo[None, c0:c0 + 64, 64:80])
            tld = ctx.dma("sync", rowbuf[0:1, 1024:2048]
                          .rearrange("o (t n) -> o t n", n=16),
                          dbcMo[None, c0:c0 + 64, 80:96])
            ctx.wait("tensor", "pv", ctx.tok("pv"))
            ctx.wait("tensor", "d_sync", tld)
            for j, pj in ((0, PS_B), (1, PS_B), (2, PS_C), (3, PS_C)):
                ctx.sig("tensor", lambda t, j=j, pj=pj: t.matmul(
                    pj[:, (j % 2) * 512:(j % 2 + 1) * 512], onesc[0:1, :],
                    rowbuf[0:1, j * 512:(j + 1) * 512],
                    start=True, stop=True))
            tbc = ctx.tok("pt")
            ctx.wait("vector", "d_sync", tld)
            ctx.wait("vector", "pt", tbc)
            ctx.sig("vector", lambda v: v.tensor_copy(out=bbuf[:, :],
                                                      in_=PS_B[:, 0:1024]))
            tbe = ctx.sig("vector", lambda v: v.tensor_copy(
                out=cbuf[:, :], in_=PS_C[:, 0:1024]))
            ctx.sig("vector", lambda v: v.tensor_tensor(
                out=xdtb[:, :, :], in0=dtb.rearrange("p g t -> p t g"),
                in1=xsb5.rearrange("p g t -> p t g"), op=OP.mult))
            ctx.sig("vector", lambda v: v.tensor_tensor(
                out=zb[:, :, :, :],
                in0=dtb.rearrange("p g t -> p t g")[:, :, :, None]
                .broadcast_to([128, 64, 4, 16]),
                in1=negA[:, None, :, :].broadcast_to([128, 64, 4, 16]),
                op=OP.mult))
            tcl = ctx.sig("vector", lambda v: v.tensor_scalar(
                out=zb2[:, :, :, :], in0=zb[:, :, :, :],
                scalar1=-1.0, scalar2=-10.0, op0=OP.mult, op1=OP.max))
            ctx.wait("scalar", "pv", tcl)
            tex = ctx.sig("scalar", lambda s: s.activation(
                out=dAb[:, :, :, :].rearrange("p a b c -> p (a b c)"),
                in_=zb2[:, :, :, :].rearrange("p a b c -> p (a b c)"),
                func=AF.Exp))
            ctx.sig("vector", lambda v: v.tensor_tensor(
                out=ub[:, :, :, :],
                in0=xdtb[:, :, :, None].broadcast_to([128, 64, 4, 16]),
                in1=bbuf[:, :].rearrange("p (t n) -> p t n", n=16)
                [:, :, None, :].broadcast_to([128, 64, 4, 16]),
                op=OP.mult))
            ctx.wait("vector", "ps", tex)
            dAf = dAb.rearrange("p t a b -> p t (a b)")
            ubf = ub.rearrange("p t a b -> p t (a b)")
            shf = sh.rearrange("p w t a b -> p w t (a b)")
            szf = szero.rearrange("p a b -> p (a b)")
            saf = sa.rearrange("p a b -> p (a b)")
            sbf = sb.rearrange("p a b -> p (a b)")
            for t_ in range(64):
                if ci == 0 and t_ == 0:
                    prev = szf
                elif t_ == 0:
                    prev = shf[:, 1 - pb, 63, :]
                else:
                    prev = shf[:, pb, t_ - 1, :]
                for lo, hi in ((0, 32), (32, 64)):
                    ctx.do("vector",
                           lambda v, t_=t_, prev=prev, lo=lo, hi=hi:
                           v.tensor_tensor(
                               out=saf[:, lo:hi], in0=prev[:, lo:hi],
                               in1=dAf[:, t_, lo:hi], op=OP.mult))
                for lo, hi in ((0, 32), (32, 64)):
                    ctx.do("vector", lambda v, t_=t_, lo=lo, hi=hi:
                           v.tensor_tensor(
                               out=sbf[:, lo:hi], in0=saf[:, lo:hi],
                               in1=ubf[:, t_, lo:hi], op=OP.add))
                for lo, hi in ((0, 32), (32, 64)):
                    ctx.do("vector", lambda v, t_=t_, pb=pb, lo=lo, hi=hi:
                           v.tensor_scalar(
                               out=shf[:, pb, t_, lo:hi], in0=sbf[:, lo:hi],
                               scalar1=-10.0, scalar2=10.0,
                               op0=OP.max, op1=OP.min))
            ctx.sig("vector", lambda v, pb=pb: v.tensor_tensor(
                out=yt4[:, :, :, :], in0=sh[:, pb, :, :, :],
                in1=cbuf[:, :].rearrange("p (t n) -> p t n", n=16)
                [:, :, None, :].broadcast_to([128, 64, 4, 16]),
                op=OP.mult))
            ctx.wait("vector", "d_gpsimd", ctx.tok("d_gpsimd"))
            tyr = ctx.sig("vector", lambda v: v.tensor_reduce(
                out=ybg.rearrange("p g t -> p t g"), in_=yt4[:, :, :, :],
                axis=AX, op=OP.add))
            ctx.wait("gpsimd", "pv", tyr)
            for g in range(4):
                ctx.dma("gpsimd", yD[g * 128:(g + 1) * 128, c0:c0 + 64],
                        ybg[:, g, :])
            if dbg and ci < 2:
                ctx.dma("gpsimd", dbgA[ci, :, :],
                        dAb.rearrange("p a b c -> p (a b c)"))
                ctx.dma("gpsimd", dbgU[ci, :, :],
                        ub.rearrange("p a b c -> p (a b c)"))
                tgs = ctx.dma("gpsimd", dbgS[ci, :, :],
                              sh[:, pb].rearrange("p a b c -> p (a b c)"))
                ctx.wait("vector", "d_gpsimd", tgs)
                ctx.wait("scalar", "d_gpsimd", tgs)
        ar.release()
        ctx.barrier()

        ar.mark()
        yb6 = ar.alloc([128, 4, 512], F32)
        xs6 = ar.alloc([128, 4, 512], F32)
        sz6 = ar.alloc([128, 4, 512], F32)
        y3b = ar.alloc([128, 4, 512], BF16)
        ev6 = ar.alloc([128, 512], F32)
        for tcol in range(16):
            c0 = tcol * 512
            qrt, tw_ = c0 // TC, c0 % TC
            ctx.wait("sync", "pv", ctx.tok("pv"))
            for g in range(4):
                ctx.dma("sync", yb6[:, g, :],
                        yD[g * 128:(g + 1) * 128, c0:c0 + 512])
                ctx.dma("sync", xs6[:, g, :],
                        xsD[g * 128:(g + 1) * 128, c0:c0 + 512])
                tld = ctx.dma("sync", sz6[:, g, :],
                              szD[g * 128:(g + 1) * 128, c0:c0 + 512])
            ctx.wait("vector", "d_sync", tld)
            for g in range(4):
                ctx.sig("vector", lambda v, g=g: v.scalar_tensor_tensor(
                    out=yb6[:, g, :], in0=xs6[:, g, :],
                    scalar=dcolb[:, g:g + 1], in1=yb6[:, g, :],
                    op0=OP.mult, op1=OP.add))
            ctx.sig("vector", lambda v: v.tensor_tensor(
                out=yb6[:, :, :].rearrange("p a b -> p (a b)"),
                in0=yb6[:, :, :].rearrange("p a b -> p (a b)"),
                in1=sz6[:, :, :].rearrange("p a b -> p (a b)"), op=OP.mult))
            tvc = ctx.sig("vector", lambda v: v.tensor_copy(
                out=y3b[:, :, :], in_=yb6[:, :, :]))
            ctx.wait("tensor", "pv", tvc)
            for do in range(8):
                pj = PS_A if do % 2 == 0 else PS_D
                tmm = ctx.sig("tensor", lambda t, do=do, pj=pj: [
                    t.matmul(pj[:, 0:512],
                             woutb[:, g, do * 128:(do + 1) * 128],
                             y3b[:, g, :], start=(g == 0), stop=(g == 3))
                    for g in range(4)][-1])
                ctx.wait("vector", "pt", tmm)
                tev = ctx.sig("vector", lambda v, pj=pj: v.tensor_copy(
                    out=ev6[:, :], in_=pj[:, 0:512]))
                ctx.wait("gpsimd", "pv", tev)
                tg = ctx.dma("gpsimd",
                             RSi[qrt, do * 128:(do + 1) * 128,
                                 tw_:tw_ + 512], ev6[:, :])
                ctx.wait("vector", "d_gpsimd", tg)
                ctx.wait("tensor", "pv", tev)
        ar.release()
        ctx.barrier()

        ctx.wait("gpsimd", "d_gpsimd", ctx.tok("d_gpsimd"))
        tc3 = ctx.sig("gpsimd", lambda g: g.collective_compute(
            "ReduceScatter", OP.add, ins=[RSi.opt()], outs=[attnD.opt()],
            replica_groups=[[0, 1, 2, 3], [4, 5, 6, 7]]))
        ctx.wait("gpsimd", "pp", tc3)
        if dbg:
            ctx.dma("gpsimd", attnDX[:, :], attnD[:, :])
            ctx.wait("gpsimd", "d_gpsimd", ctx.tok("d_gpsimd"))
        ctx.barrier()

        ar.mark()
        xqfm = ar.alloc([128, 8, 2048], BF16)
        raw = ar.alloc([128, 1024], F32)
        absn = ar.alloc([128, 1024], F32)
        absn2 = ar.alloc([128, 1024], F32)
        sgn = ar.alloc([128, 1024], F32)
        scl = ar.alloc([128, 64], F32)
        scx = ar.alloc([128, 1024], F32)
        scx2 = ar.alloc([128, 1024], F32)
        rsx = ar.alloc([128, 1024], F32)
        u0b = ar.alloc([128, 1024], F32)
        u1b = ar.alloc([128, 1024], F32)
        u2b = ar.alloc([128, 1024], F32)
        u3b = ar.alloc([128, 1024], F32)
        b0b = ar.alloc([128, 1024], F32)
        b1b = ar.alloc([128, 1024], F32)
        dmyb = ar.alloc([128, 1024], F32)
        tmf = ar.alloc([128, 1024], F32)
        evt = ar.alloc([128, 128], F32)

        MIDS = [0.25, 0.75, 1.25, 1.75, 2.5, 3.5, 5.0, 5.0]
        DGS = [0.5, 0.5, 0.5, 0.5, 1.0, 1.0, 1.0, 1.0]

        def qdq_sb(src_ap, dst_ap, cols):
            nb = cols // 16
            xb_ = src_ap.rearrange("p (nb k) -> p nb k", k=16)
            ctx.wait("scalar", "pv", ctx.tok("pv"))

            def V(fn):
                return ctx.sig("vector", fn)

            def dmy():
                ctx.do("vector", lambda v: v.tensor_copy(
                    out=dmyb[:, 0:cols], in_=src_ap))

            V(lambda v: v.tensor_reduce(
                out=scl[:, 0:nb], in_=xb_, axis=AX, op=OP.max,
                apply_absolute_value=True))
            V(lambda v: v.tensor_copy(
                out=scx[:, 0:cols].rearrange("p (nb k) -> p nb k", k=16),
                in_=scl[:, 0:nb, None].broadcast_to([128, nb, 16])))
            dmy()
            V(lambda v: v.tensor_scalar(
                out=scx2[:, 0:cols], in0=scx[:, 0:cols],
                scalar1=1.0 / 6.0, scalar2=1e-12, op0=OP.mult, op1=OP.max))
            ta2 = ctx.sig("scalar", lambda s: s.activation(
                out=absn[:, 0:cols], in_=src_ap, func=AF.Abs))
            ts3 = ctx.sig("scalar", lambda s: s.activation(
                out=sgn[:, 0:cols], in_=src_ap, func=AF.Sign))
            dmy()
            V(lambda v: v.reciprocal(out=rsx[:, 0:cols], in_=scx2[:, 0:cols]))
            dmy()
            ctx.wait("vector", "ps", ta2)
            V(lambda v: v.tensor_tensor(
                out=absn2[:, 0:cols], in0=absn[:, 0:cols],
                in1=rsx[:, 0:cols], op=OP.mult))
            dmy()

            def st(dst, mid, dg):
                V(lambda v, dst=dst, mid=mid, dg=dg: v.tensor_scalar(
                    out=dst[:, 0:cols], in0=absn2[:, 0:cols],
                    scalar1=mid, scalar2=dg, op0=OP.is_gt, op1=OP.mult))

            def ad(dst, a, bb):
                V(lambda v, dst=dst, a=a, bb=bb: v.tensor_tensor(
                    out=dst[:, 0:cols], in0=a[:, 0:cols], in1=bb[:, 0:cols],
                    op=OP.add))

            st(u0b, 0.25, 0.5)
            st(u1b, 0.75, 0.5)
            st(u2b, 1.25, 0.5)
            ad(b0b, u0b, u1b)
            st(u3b, 1.75, 0.5)
            ad(b1b, b0b, u2b)
            st(u0b, 2.5, 1.0)
            ad(b0b, b1b, u3b)
            st(u1b, 3.5, 1.0)
            ad(b1b, b0b, u0b)
            st(u2b, 5.0, 1.0)
            ad(b0b, b1b, u1b)
            st(u3b, 5.0, 1.0)
            ad(b1b, b0b, u2b)
            dmy()
            ad(b0b, b1b, u3b)
            ctx.wait("vector", "ps", ts3)
            dmy()
            V(lambda v: v.tensor_tensor(
                out=b1b[:, 0:cols], in0=sgn[:, 0:cols], in1=b0b[:, 0:cols],
                op=OP.mult))
            dmy()
            return V(lambda v: v.tensor_tensor(
                out=dst_ap, in0=b1b[:, 0:cols], in1=scx2[:, 0:cols],
                op=OP.mult))

        for tt in range(16):
            ctx.wait("sync", "pt", ctx.tok("pt"))
            ctx.wait("sync", "pv", ctx.tok("pv"))
            for ki in range(8):
                ctx.dma("sync", xld[:, ki % 2, 0:128],
                        xnF[ki * 128:(ki + 1) * 128, tt * 128:(tt + 1) * 128])
                tld = ctx.tok("d_sync")
                ctx.wait("tensor", "d_sync", tld)
                ttr = ctx.sig("tensor", lambda t, ki=ki: t.transpose(
                    PS_D[:, 0:128], xld[:, ki % 2, 0:128], eyeb[:, :]))
                ctx.wait("vector", "pt", ttr)
                tve = ctx.sig("vector", lambda v, ki=ki: v.tensor_copy(
                    out=tmf[:, ki * 128:(ki + 1) * 128], in_=PS_D[:, 0:128]))
                ctx.wait("tensor", "pv", tve)
                ctx.wait("sync", "pv", tve)
            tq = qdq_sb(tmf[:, :], raw[:, :], 1024)
            ctx.wait("tensor", "pv", tq)
            for ki in range(8):
                ttr = ctx.sig("tensor", lambda t, ki=ki: t.transpose(
                    PS_D[:, 0:128],
                    raw[:, ki * 128:(ki + 1) * 128],
                    eyeb[:, :]))
                ctx.wait("vector", "pt", ttr)
                tve = ctx.sig("vector", lambda v, ki=ki, tt=tt: v.tensor_copy(
                    out=xqfm[:, ki, tt * 128:(tt + 1) * 128],
                    in_=PS_D[:, 0:128]))
                ctx.wait("tensor", "pv", tve)

        if dbg:
            ctx.wait("gpsimd", "pv", ctx.tok("pv"))
            for ki in range(8):
                for tc_ in range(4):
                    ctx.dma("gpsimd",
                            xqDX[ki * 128:(ki + 1) * 128,
                                 tc_ * 512:(tc_ + 1) * 512],
                            xqfm[:, ki, tc_ * 512:(tc_ + 1) * 512])
            ctx.wait("gpsimd", "d_gpsimd", ctx.tok("d_gpsimd"))
            ctx.barrier()

        wgc = ar.alloc([128, 8, 512], BF16)
        wuc = ar.alloc([128, 8, 512], BF16)
        gs6 = ar.alloc([128, 512], F32)
        h6 = ar.alloc([128, 512], F32)
        hqf = ar.alloc([128, 512], F32)
        HCH = [512, 512, 512, 512, 512, 256]
        for hc in range(6):
            h0 = hc * 512
            hw = HCH[hc]
            ctx.wait("sync", "pt", ctx.tok("pt"))
            for ki in range(8):
                ctx.dma("sync", wgc[:, ki, 0:hw],
                        wg2T[ki * 128:(ki + 1) * 128, h0:h0 + hw])
                ctx.dma("sync", wuc[:, ki, 0:hw],
                        wu2T[ki * 128:(ki + 1) * 128, h0:h0 + hw])
            tldw = ctx.tok("d_sync")
            ctx.wait("tensor", "d_sync", tldw)
            for tt in range(16):
                tmm = ctx.sig("tensor", lambda t, tt=tt, hw=hw: [
                    t.matmul(PS_A[:, w * 512:w * 512 + hw],
                             xqfm[:, ki, tt * 128:(tt + 1) * 128],
                             (wgc if w == 0 else wuc)[:, ki, 0:hw],
                             start=(ki == 0), stop=(ki == 7))
                    for w in range(2) for ki in range(8)][-1])
                ctx.wait("scalar", "pt", tmm)
                tsl = ctx.sig("scalar", lambda s, hw=hw: s.activation(
                    out=gs6[:, 0:hw], in_=PS_A[:, 0:hw], func=AF.Silu))
                ctx.wait("vector", "ps", tsl)
                ctx.wait("vector", "pt", tmm)
                tvh = ctx.sig("vector", lambda v, hw=hw: v.tensor_tensor(
                    out=h6[:, 0:hw], in0=gs6[:, 0:hw],
                    in1=PS_A[:, 512:512 + hw], op=OP.mult))
                ctx.wait("tensor", "pv", tvh)
                tq = qdq_sb(h6[:, 0:hw], hqf[:, 0:hw], hw)
                ctx.wait("tensor", "pv", tq)
                for j in range(hw // 128):
                    ttr = ctx.sig("tensor", lambda t, j=j: t.transpose(
                        PS_D[:, 0:128], hqf[:, j * 128:(j + 1) * 128],
                        eyeb[:, :]))
                    ctx.wait("vector", "pt", ttr)
                    tve = ctx.sig("vector", lambda v: v.tensor_copy(
                        out=evt[:, :].bitcast(BF16)[:, 0:128],
                        in_=PS_D[:, 0:128]))
                    ctx.wait("gpsimd", "pv", tve)
                    tg = ctx.dma("gpsimd",
                                 hqD[h0 + j * 128:h0 + (j + 1) * 128,
                                     tt * 128:(tt + 1) * 128],
                                 evt[:, :].bitcast(BF16)[:, 0:128])
                    ctx.wait("vector", "d_gpsimd", tg)
                    ctx.wait("tensor", "pv", tve)
        ar.release()
        ctx.barrier()

        ar.mark()
        wdb = ar.alloc([128, 22, 1024], BF16)
        hq8 = ar.alloc([128, 22, 512], BF16)
        att6 = ar.alloc([128, 512], F32)
        x26 = ar.alloc([128, 512], F32)
        x3ev = ar.alloc([128, 512], F32)
        for ki in range(22):
            ctx.dma("sync", wdb[:, ki, :],
                    wd2T[ki * 128:(ki + 1) * 128, :])
        ctx.wait("tensor", "d_sync", ctx.tok("d_sync"))
        for tcol in range(4):
            c0 = tcol * 512
            ctx.wait("sync", "pt", ctx.tok("pt"))
            for ki in range(22):
                ctx.dma("sync", hq8[:, ki, :],
                        hqD[ki * 128:(ki + 1) * 128, c0:c0 + 512])
            tld = ctx.tok("d_sync")
            ctx.wait("tensor", "d_sync", tld)
            for do in range(8):
                pj = PS_A if do % 2 == 0 else PS_B
                tmm = ctx.sig("tensor", lambda t, do=do, pj=pj: [
                    t.matmul(pj[:, 0:512], wdb[:, ki, do * 128:(do + 1) * 128],
                             hq8[:, ki, :], start=(ki == 0), stop=(ki == 21))
                    for ki in range(22)][-1])
                ctx.wait("sync", "pv", ctx.tok("pv"))
                tx1 = ctx.dma("sync", att6[:, :],
                              attnD[do * 128:(do + 1) * 128, c0:c0 + 512])
                tx2 = ctx.dma("sync", x26[:, :],
                              x2T[do * 128:(do + 1) * 128, c0:c0 + 512])
                ctx.wait("vector", "pt", tmm)
                ctx.wait("vector", "d_sync", tx2)
                ctx.sig("vector", lambda v, pj=pj: v.tensor_tensor(
                    out=x3ev[:, :], in0=pj[:, 0:512], in1=att6[:, :],
                    op=OP.add))
                tv3 = ctx.sig("vector", lambda v: v.tensor_tensor(
                    out=qsc[:, :], in0=x3ev[:, :], in1=x26[:, :], op=OP.add))
                ctx.wait("gpsimd", "pv", tv3)
                ctx.wait("sync", "pv", tv3)
                tg = ctx.dma("gpsimd", x3T[do * 128:(do + 1) * 128,
                                           c0:c0 + 512], qsc[:, :])
                ctx.wait("vector", "d_gpsimd", tg)
                ctx.wait("tensor", "pv", tv3)
        ar.release()
        ctx.barrier()

        def xn3w(dt, tcol, src_ap):
            tvx = ctx.sig("vector", lambda v: v.tensor_copy(
                out=x2s[:, :].bitcast(BF16)[:, 0:512], in_=src_ap))
            ctx.wait("gpsimd", "pv", tvx)
            tst = ctx.dma("gpsimd",
                          xn3T[dt * 128:(dt + 1) * 128,
                               tcol * 512:(tcol + 1) * 512],
                          x2s[:, :].bitcast(BF16)[:, 0:512])
            ctx.wait("vector", "d_gpsimd", tst)

        emit_ln_fm(ctx, PS_A, PS_B, onesc, rowbuf, xld, x2s, qsc,
                   x3T, 0, xn3w, 4, 1e-5)
        ctx.barrier()
        ctx.wait("gpsimd", "d_gpsimd", ctx.tok("d_gpsimd"))
        with nc.Block() as block:
            ctx.replay(block)
    return nc


def l2_in_maps(inp, prep, x2T_list, xn2_list, xn2F_list):
    Wc = np.asarray(inp["L2_attn_W_conv"], np.float32)
    bc = np.asarray(inp["L2_attn_b_conv"], np.float32)
    Wx = np.asarray(inp["L2_attn_W_xproj"], np.float32)
    Wdt = np.asarray(inp["L2_attn_W_dt"], np.float32)
    bdt = np.asarray(inp["L2_attn_b_dt"], np.float32)
    Alog = np.asarray(inp["L2_attn_A_log"], np.float32)
    Dv = np.asarray(inp["L2_attn_D"], np.float32)
    Win = np.asarray(inp["L2_attn_W_in"], np.float32)
    Wo = np.asarray(inp["L2_attn_W_out"], np.float32)
    maps = []
    for c in range(NCORE):
        b, q = c // 4, c % 4
        cs = q * 512
        chid = cs + (np.arange(4)[None, :] * 128 + np.arange(128)[:, None])
        xnfull = np.concatenate([np.asarray(xn2_list[b * 4 + j])
                                 for j in range(4)], axis=1)
        m = {
            "xnfT2": np.ascontiguousarray(xnfull),
            "xnF": np.asarray(xn2F_list[c]),
            "x2T": np.asarray(x2T_list[c]),
            "winX": np.ascontiguousarray(Win[:, cs:cs + 512]),
            "winZ": np.ascontiguousarray(Win[:, 2048 + cs:2048 + cs + 512]),
            "wcv": np.ascontiguousarray(
                Wc[:, chid].transpose(1, 2, 0).reshape(128, 16)),
            "bcv": np.ascontiguousarray(bc[chid]),
            "wxp": np.ascontiguousarray(Wx[cs:cs + 512]),
            "wdtw": np.ascontiguousarray(Wdt[:, cs:cs + 512]),
            "bdtc": np.ascontiguousarray(bdt[chid]),
            "alog": np.ascontiguousarray(
                Alog[chid].reshape(128, 64)),
            "dcolI": np.ascontiguousarray(Dv[chid]),
            "woutI": np.ascontiguousarray(Wo[cs:cs + 512]),
            "wg2T": prep["wg2T"],
            "wu2T": prep["wu2T"],
            "wd2T": prep["wd2T"],
            "eye": np.eye(128, dtype=np.float32),
        }
        maps.append(m)
    return maps


def kernel(**inp):
    prep = run_W0(inp)

    nc0 = build_L0()
    res0 = run_bass_kernel_spmd(nc0, l0_in_maps(inp, prep),
                                core_ids=list(range(NCORE)))
    x1T = [np.asarray(res0.results[c]["x1T"]) for c in range(NCORE)]
    xn1 = [np.asarray(res0.results[c]["xn1T"]) for c in range(NCORE)]

    nc1 = build_L1()
    res1 = run_bass_kernel_spmd(nc1, l1_in_maps(inp, prep, x1T, xn1),
                                core_ids=list(range(NCORE)))
    x2T = [np.asarray(res1.results[c]["x2T"]) for c in range(NCORE)]
    xn2 = [np.asarray(res1.results[c]["xn2T"]) for c in range(NCORE)]
    xn2F = [np.asarray(res1.results[c]["xn2F"]) for c in range(NCORE)]

    nc2 = build_L2()
    res2 = run_bass_kernel_spmd(nc2, l2_in_maps(inp, prep, x2T, xn2, xn2F),
                                core_ids=list(range(NCORE)))
    x3T = [np.asarray(res2.results[c]["x3T"]) for c in range(NCORE)]
    xn3 = [np.asarray(res2.results[c]["xn3T"]) for c in range(NCORE)]

    nc3 = build_L3()
    res3 = run_bass_kernel_spmd(nc3, l3_in_maps(inp, x3T, xn3),
                                core_ids=list(range(NCORE)))
    out = np.zeros((2, T, D), np.float32)
    for c in range(NCORE):
        b, q = c // 4, c % 4
        out[b, q * TC:(q + 1) * TC] = np.asarray(res3.results[c]["x4T"]).T
    return out
